# version 03 (final)
# baseline (speedup 1.0000x reference)
import numpy as np
from contextlib import ExitStack
import concourse.bass as bass
import concourse.mybir as mybir
from concourse.bass_utils import run_bass_kernel_spmd
import ml_dtypes

F32 = mybir.dt.float32
BF16 = mybir.dt.bfloat16
U32 = mybir.dt.uint32
AF = mybir.ActivationFunctionType
ALU = mybir.AluOpType

T = 4096
D = 1024
NT = T // 128
NCH = D // 128

ENGINES = ("sync", "vector", "scalar", "tensor", "gpsimd")


class SemPlan:
    def __init__(self, nc, ctx, name):
        self.sem = ctx.enter_context(nc.semaphore(name))
        self._v = 0
        self.base = 0

    @property
    def v(self):
        return self._v - self.base

    def bump(self, n=1):
        self._v += n
        return self.v


class Prog:
    def __init__(self, nc, ctx):
        self.nc = nc
        self.ctx = ctx
        self.q = {e: [] for e in ENGINES}
        self._sem_id = 0
        self._ticks = {}
        self.arena = None
        self.arena_off = 0
        self.arena_base = 0
        self.psum_pool = None
        self.gate = None
        self._gated = set()

    def arena_init(self, words, persist_words=0):
        self.arena = self.ctx.enter_context(
            self.nc.sbuf_tensor("arena", [128, words], F32))
        self.arena_base = persist_words
        self.arena_off = persist_words
        self.arena_persist_off = 0

    def asb(self, name, shape, dt=F32, persist=False):
        bpe = 2 if dt == BF16 else 4
        words = (shape[1] * bpe + 3) // 4
        if persist:
            off = self.arena_persist_off
            self.arena_persist_off += words
            assert self.arena_persist_off <= self.arena_base, "persist arena overflow"
        else:
            off = self.arena_off
            self.arena_off += words
            assert self.arena_off <= self.arena.shape[1], (
                f"arena overflow at {name}: off={self.arena_off} cap={self.arena.shape[1]}")
        v = self.arena[:, off:off + words]
        if dt != F32:
            v = v.bitcast(dt)
        if shape[0] != 128:
            v = v[0:shape[0], :]
        if v.free_size() != shape[1]:
            v = v[:, 0:shape[1]]
        return v

    def arena_reset(self):
        self.arena_off = self.arena_base

    def psum_init(self):
        self.psum_pool = dict(
            B0=self.ctx.enter_context(self.nc.psum_tensor("ppB0", [128, 512], BF16)),
            B1=self.ctx.enter_context(self.nc.psum_tensor("ppB1", [128, 512], BF16)),
            F0=self.ctx.enter_context(self.nc.psum_tensor("ppF0", [128, 512], F32)),
            F1=self.ctx.enter_context(self.nc.psum_tensor("ppF1", [128, 512], F32)),
            F2=self.ctx.enter_context(self.nc.psum_tensor("ppF2", [128, 512], F32)),
            F3=self.ctx.enter_context(self.nc.psum_tensor("ppF3", [128, 512], F32)),
            F4=self.ctx.enter_context(self.nc.psum_tensor("ppF4", [128, 512], F32)),
            F5=self.ctx.enter_context(self.nc.psum_tensor("ppF5", [128, 512], F32)),
        )

    def psum_for(self, name, shape, dt):
        k = int(name[-1]) if name[-1].isdigit() else 0
        base = None
        if dt == BF16:
            base = self.psum_pool[f"B{k}"]
        elif name.startswith(("ptrf", "mptr")):
            base = self.psum_pool[f"F{2 + k}"]
        elif name.startswith(("pq", "pqk", "pc", "pm", "mpm")):
            base = self.psum_pool[f"F{k}"]
        elif name.startswith("pav"):
            base = self.psum_pool[f"F{2 + k}"]
        elif name.startswith("pb1"):
            base = self.psum_pool["F2"]
        elif name.startswith(("ppr", "prpl")):
            base = self.psum_pool[f"F{4 + k}"]
        else:
            raise KeyError(name)
        v = base
        if shape[0] != 128 or shape[1] != 512:
            v = v[0:shape[0], 0:shape[1]]
        return v

    def set_gate(self, sem, val):
        self.gate = (sem, val)
        self._gated = set()

    def sem(self, name=None):
        if getattr(self, "pooled", False):
            i = self._blk_sem_i
            self._blk_sem_i += 1
            while len(self._pool) <= i:
                self._sem_id += 1
                self._pool.append(SemPlan(self.nc, self.ctx, f"pool_{self._sem_id}"))
            s = self._pool[i]
            self._blk_used = max(self._blk_used, i + 1)
            return s
        self._sem_id += 1
        return SemPlan(self.nc, self.ctx, f"{name or 's'}_{self._sem_id}")

    def gsem(self, name=None):
        self._sem_id += 1
        return SemPlan(self.nc, self.ctx, f"g{name or 's'}_{self._sem_id}")

    def pool_init(self):
        self.pooled = True
        self._pool = []
        self._blk_sem_i = 0
        self._blk_used = 0
        self._epoch = None

    def new_block(self, gate):
        if self._epoch is None:
            self._sem_id += 1
            self._epoch = SemPlan(self.nc, self.ctx, "epoch")
        used = [self._pool[i] for i in range(self._blk_used)]
        ep = self._epoch
        ep.bump(1)
        finals = [(s.sem, s._v) for s in used]
        gate_abs = None if gate is None else (gate[0].sem, gate[1] + gate[0].base)
        def fb(e, gate_abs=gate_abs, finals=finals):
            if gate_abs is not None:
                e.wait_ge(gate_abs[0], gate_abs[1])
            for sm, v in finals:
                if v > 0:
                    e.wait_ge(sm, v)
            e.nop().then_inc(ep.sem, 1)
        self.q["sync"].append(fb)
        self.gate = (ep, ep._v)
        self._gated = {"sync"}
        for s in used:
            s.base = s._v
        self._blk_sem_i = 0
        self.arena_reset()

    def on(self, eng, fn):
        if self.gate is not None and eng not in self._gated:
            self._gated.add(eng)
            g = self.gate
            self.q[eng].append(lambda e, g=g: e.wait_ge(g[0].sem, g[1]))

        self.q[eng].append(fn)

    def op(self, eng, fn, waits=(), inc=None, inc_n=1):
        if self.gate is not None and eng not in self._gated:
            self._gated.add(eng)
            g = self.gate
            self.q[eng].append(lambda e, g=g: e.wait_ge(g[0].sem, g[1]))
        w = [(s.sem, v + s.base) for (s, v) in waits]
        if inc is not None:
            inc.bump(inc_n)
        def run(e, fn=fn, w=w, inc=inc, inc_n=inc_n):
            for s, v in w:
                e.wait_ge(s, v)
            inst = fn(e)
            if inc is not None:
                inst.then_inc(inc.sem, inc_n)
        self.q[eng].append(run)
        return inc.v if inc is not None else None

    def ss(self, eng, e, inst):
        if eng not in self._ticks:
            self._sem_id += 1
            self._ticks[eng] = [SemPlan(self.nc, self.ctx, f"tick_{eng}_{self._sem_id}"), 0]
        t = self._ticks[eng]
        t[1] += 1
        inst.then_inc(t[0].sem, 1)
        e.wait_ge(t[0].sem, t[1])
        if hasattr(e, "engine_nop"):
            e.engine_nop()

    def build(self):
        block = self.ctx.enter_context(self.nc.Block())

        @block.sync
        def _(e):
            for fn in self.q["sync"]:
                fn(e)

        @block.vector
        def _(e):
            for fn in self.q["vector"]:
                fn(e)

        @block.scalar
        def _(e):
            for fn in self.q["scalar"]:
                fn(e)

        @block.tensor
        def _(e):
            for fn in self.q["tensor"]:
                fn(e)

        @block.gpsimd
        def _(e):
            for fn in self.q["gpsimd"]:
                fn(e)



def _bf(x):
    return np.asarray(x, np.float32).astype(ml_dtypes.bfloat16)


def _split_hi_lo(x):
    x = np.asarray(x, np.float32)
    hi = x.astype(ml_dtypes.bfloat16).astype(np.float32)
    lo = (x - hi).astype(ml_dtypes.bfloat16)
    return hi.astype(ml_dtypes.bfloat16), lo


_CONSTS = None


def get_consts():
    global _CONSTS
    if _CONSTS is None:
        n = D
        kk = np.arange(n)
        Fm = np.exp(-2j * np.pi * np.outer(kk, kk) / n)
        Gm = np.conj(Fm) / n
        c = {}
        c["FrH"], c["FrL"] = _split_hi_lo(Fm.real)
        c["FinH"], c["FinL"] = _split_hi_lo(-Fm.imag)
        c["FiH"], c["FiL"] = _split_hi_lo(Fm.imag)
        c["GrH"], c["GrL"] = _split_hi_lo(Gm.real)
        c["GinH"], c["GinL"] = _split_hi_lo(-Gm.imag)
        c["identb"] = _bf(np.eye(128))
        c["identf"] = np.eye(128, dtype=np.float32)
        c["ones_row_b"] = _bf(np.ones((1, 128)))
        HD = 64
        inv = 1.0 / np.array(10000.0) ** (np.arange(0, HD, 2) / HD)
        fr = np.arange(T)[:, None] * inv[None, :]
        c["rope_cos"] = np.cos(fr).T.astype(np.float32).copy()
        c["rope_sin"] = np.sin(fr).T.astype(np.float32).copy()
        _CONSTS = c
    return _CONSTS



def sbt(nc, ctx, name, shape, dt=F32):
    pr = getattr(nc, "_fused_prog", None)
    if pr is not None:
        return pr.asb(name, shape, dt)
    h = ctx.enter_context(nc.sbuf_tensor(name, shape, dt))
    return h.ap()


def pst(nc, ctx, name, shape, dt=F32):
    pr = getattr(nc, "_fused_prog", None)
    if pr is not None:
        return pr.psum_for(name, shape, dt)
    h = ctx.enter_context(nc.psum_tensor(name, shape, dt))
    return h.ap()


def emit_load_x(p, x_hbm, x_sb, sem, queues=("sync", "scalar", "sync", "scalar")):
    per = NT // len(queues)
    src = x_hbm.rearrange("(i p) c -> p i c", p=128)
    dst = x_sb[:].rearrange("p (i c) -> p i c", c=D)
    for qi, q in enumerate(queues):
        i0 = qi * per
        sem.bump(16)
        def fn(e, i0=i0):
            e.dma_start(out=dst[:, i0:i0 + per, :], in_=src[:, i0:i0 + per, :]).then_inc(sem.sem, 16)
        p.on(q, fn)
    return sem.v


def emit_store_x(p, x_sb, x_hbm, sem, wait_sem, wait_v,
                 queues=("sync", "scalar", "sync", "scalar")):
    per = NT // len(queues)
    dst = x_hbm.rearrange("(i p) c -> p i c", p=128)
    src = x_sb[:].rearrange("p (i c) -> p i c", c=D)
    wv_abs = wait_v + wait_sem.base
    for qi, q in enumerate(queues):
        i0 = qi * per
        sem.bump(16)
        def fn(e, i0=i0, wv_abs=wv_abs):
            e.wait_ge(wait_sem.sem, wv_abs)
            e.dma_start(out=dst[:, i0:i0 + per, :], in_=src[:, i0:i0 + per, :]).then_inc(sem.sem, 16)
        p.on(q, fn)
    p.on("sync", lambda e, v=sem._v: e.wait_ge(sem.sem, v))


def emit_ln_stats(p, nc, ctx, name, x_sb, xrdy, xrdy_v, rstd_hook=None, scr_bf=None):
    ssum = sbt(nc, ctx, f"{name}_sum", [128, NT])
    sssq = sbt(nc, ctx, f"{name}_ssq", [128, NT])
    smu = sbt(nc, ctx, f"{name}_mu", [128, NT])
    srstd = sbt(nc, ctx, f"{name}_rstd", [128, NT])
    snmu = sbt(nc, ctx, f"{name}_nmu", [128, NT])
    if scr_bf is None:
        scr_bf = sbt(nc, ctx, f"{name}_scr", [128, D], BF16)
    s_stat = p.sem()
    s_sqrt = p.sem()
    s_post = p.sem()
    xv = x_sb[:].rearrange("p (i c) -> p i c", c=D)
    xr_abs = xrdy_v + xrdy.base
    for i in range(NT):
        s_stat.bump()
        def fn(e, i=i, xr_abs=xr_abs):
            if i == 0:
                e.wait_ge(xrdy.sem, xr_abs)
            e.activation(out=scr_bf[:, 0:D], in_=xv[:, i, :], func=AF.Copy,
                         accum_out=ssum[:, i:i + 1])
            e.activation(out=scr_bf[:, 0:D], in_=xv[:, i, :], func=AF.Square,
                         accum_out=sssq[:, i:i + 1]).then_inc(s_stat.sem, 1)
        p.on("scalar", fn)

    _b1 = NT + s_stat.base
    def post1(e, _b1=_b1):
        e.wait_ge(s_stat.sem, _b1)
        p.ss("vector", e, e.tensor_scalar(out=smu[:], in0=ssum[:], scalar1=1.0 / D,
                                          scalar2=None, op0=ALU.mult))
        p.ss("vector", e, e.tensor_tensor(out=srstd[:], in0=smu[:], in1=smu[:], op=ALU.mult))
        p.ss("vector", e, e.tensor_scalar(out=srstd[:], in0=srstd[:], scalar1=-1e-5,
                                          scalar2=None, op0=ALU.add))
        e.scalar_tensor_tensor(out=srstd[:], in0=sssq[:], scalar=1.0 / D,
                               in1=srstd[:], op0=ALU.mult, op1=ALU.subtract).then_inc(s_post.sem, 1)
    s_post.bump()
    p.on("vector", post1)

    _b2 = 1 + s_post.base
    def post_sqrt(e, _b2=_b2):
        e.wait_ge(s_post.sem, _b2)
        e.activation(out=srstd[:], in_=srstd[:], func=AF.Sqrt).then_inc(s_sqrt.sem, 1)
    s_sqrt.bump()
    p.on("scalar", post_sqrt)

    _b3 = 1 + s_sqrt.base
    def post2(e, _b3=_b3):
        e.wait_ge(s_sqrt.sem, _b3)
        p.ss("vector", e, e.reciprocal(out=srstd[:], in_=srstd[:]))
        if rstd_hook is not None:
            rstd_hook(e, dict(mu=smu, rstd=srstd, sum=ssum, ssq=sssq))
        p.ss("vector", e, e.tensor_tensor(out=snmu[:], in0=smu[:], in1=srstd[:], op=ALU.mult))
        e.tensor_scalar(out=snmu[:], in0=snmu[:], scalar1=-1.0, scalar2=None,
                        op0=ALU.mult).then_inc(s_post.sem, 1)
    s_post.bump()
    p.on("vector", post2)
    return dict(mu=smu, rstd=srstd, nmu=snmu, s_post=s_post, sum=ssum, ssq=sssq)


def emit_ln_apply_tile(p, st, x_sb, i, out_bf_tile, s_apply, prior_wait=None):
    xv = x_sb[:].rearrange("p (i c) -> p i c", c=D)
    s_apply.bump()
    _b4 = 2 + st["s_post"].base
    def fn(e, i=i, _b4=_b4):
        if i == 0:
            e.wait_ge(st["s_post"].sem, _b4)
        if prior_wait is not None:
            prior_wait(e)
        e.tensor_scalar(out=out_bf_tile, in0=xv[:, i, :],
                        scalar1=st["rstd"][:, i:i + 1], scalar2=st["nmu"][:, i:i + 1],
                        op0=ALU.mult, op1=ALU.add).then_inc(s_apply.sem, 1)
    p.on("vector", fn)


def emit_transpose_tile(p, xn_tile_bf, ident_b, ptr_pair, dst3, slot, s_apply, a_v,
                        s_tr, s_trev, war_wait=None):
    for gidx, c4 in enumerate((0, 4)):
        s_tr.bump()
        def ftr(e, c4=c4, tvl=s_tr.v, ev=s_trev.v, evb=s_trev.base,
                av_abs=a_v + s_apply.base):
            e.wait_ge(s_apply.sem, av_abs)
            if ev >= 2:
                e.wait_ge(s_trev.sem, ev - 1 + evb)
            pb = ptr_pair[(tvl - 1) % 2]
            inst = None
            for j in range(4):
                inst = e.transpose(pb[:, j * 128:(j + 1) * 128],
                                   xn_tile_bf[:, (c4 + j) * 128:(c4 + j + 1) * 128],
                                   ident_b[:])
            inst.then_inc(s_tr.sem, 1)
        p.on("tensor", ftr)
        s_trev.bump()
        def fev(e, c4=c4, tva=s_tr.v + s_tr.base, tvl=s_tr.v, gidx=gidx):
            e.wait_ge(s_tr.sem, tva)
            if war_wait is not None and gidx == 0:
                war_wait(e)
            e.activation(out=dst3[:, c4:c4 + 4, slot * 128:(slot + 1) * 128],
                         in_=ptr_pair[(tvl - 1) % 2][:].rearrange("p (c t) -> p c t", t=128),
                         func=AF.Copy).then_inc(s_trev.sem, 1)
        p.on("scalar", fev)


def emit_weight_load(p, nc, ctx, name, w_hbm, sem, n_out=None, order="dm", dt=BF16,
                     queue="gpsimd"):
    d0, d1 = w_hbm.shape
    if order == "dm":
        dd, m = d0, d1
        src = w_hbm.rearrange("(c p) m -> p c m", p=128)
    else:
        m, dd = d0, d1
        src = w_hbm.rearrange("m (c p) -> p c m", p=128)
    ncch = dd // 128
    w_sb = sbt(nc, ctx, name, [128, ncch * m], dt)
    dst = w_sb[:].rearrange("p (c m) -> p c m", m=m)
    sem.bump(16)
    def fn(e):
        e.dma_start(out=dst[:, :, :], in_=src,
                    allow_slow_non_contiguous=(order == "md")).then_inc(sem.sem, 16)
    p.on(queue, fn)
    return w_sb



def build_prog_c12_sliced():
    nc = bass.Bass("TRN2", target_bir_lowering=False)
    ctx = ExitStack()
    Wrs = nc.dram_tensor("Wrs", [D, 128], F32, kind="ExternalInput").ap()
    Wis = nc.dram_tensor("Wis", [D, 128], F32, kind="ExternalInput").ap()
    names = ["FrH", "FrL", "FinH", "FinL", "FiH", "FiL"]
    Fap = {n: nc.dram_tensor(n, [D, D], BF16, kind="ExternalInput").ap() for n in names}
    C1o = nc.dram_tensor("C1o", [D, 128], F32, kind="ExternalOutput").ap()
    C2o = nc.dram_tensor("C2o", [D, 128], F32, kind="ExternalOutput").ap()

    p = Prog(nc, ctx)
    s_ld = p.sem("ld")
    s_spl = p.sem("spl")
    s_mm = p.sem("mm")
    s_ev = p.sem("ev")
    s_st = p.sem("st")

    Fs = {}
    for n in names:
        Fs[n] = sbt(nc, ctx, "s" + n, [128, NCH * D], BF16)
        s_ld.bump(16)
        def fn(e, n=n):
            e.dma_start(out=Fs[n][:].rearrange("p (c m) -> p c m", m=D),
                        in_=Fap[n].rearrange("(c p) m -> p c m", p=128)).then_inc(s_ld.sem, 16)
        p.on("sync", fn)
    Wf = {}
    Wh = {}
    Wl = {}
    for n, ap in (("r", Wrs), ("i", Wis)):
        Wf[n] = sbt(nc, ctx, f"sW{n}", [128, NCH * 128], F32)
        Wh[n] = sbt(nc, ctx, f"sW{n}h", [128, NCH * 128], BF16)
        Wl[n] = sbt(nc, ctx, f"sW{n}l", [128, NCH * 128], BF16)
        s_ld.bump(16)
        def fn(e, n=n, ap=ap):
            e.dma_start(out=Wf[n][:].rearrange("p (c m) -> p c m", m=128),
                        in_=ap.rearrange("(c p) m -> p c m", p=128)).then_inc(s_ld.sem, 16)
        p.on("sync", fn)
    wlscr = sbt(nc, ctx, "wlscr", [128, NCH * 128], F32)
    def fsplit(e):
        e.wait_ge(s_ld.sem, s_ld.v)
        for n in ("r", "i"):
            p.ss("vector", e, e.tensor_copy(out=Wh[n][:], in_=Wf[n][:]))
            p.ss("vector", e, e.tensor_copy(out=wlscr[:], in_=Wh[n][:]))
            p.ss("vector", e, e.tensor_sub(out=wlscr[:], in0=Wf[n][:], in1=wlscr[:]))
            p.ss("vector", e, e.tensor_copy(out=Wl[n][:], in_=wlscr[:]))
        e.engine_nop().then_inc(s_spl.sem, 1)
    s_spl.bump()
    p.on("vector", fsplit)

    pm = [pst(nc, ctx, f"pc{k}", [128, 128]) for k in range(2)]
    Co = [sbt(nc, ctx, f"sC{k}", [128, NCH * 128], F32) for k in range(2)]
    terms1 = [("FrH", "r", "h"), ("FrH", "r", "l"), ("FrL", "r", "h"),
              ("FinH", "i", "h"), ("FinH", "i", "l"), ("FinL", "i", "h")]
    terms2 = [("FrH", "i", "h"), ("FrH", "i", "l"), ("FrL", "i", "h"),
              ("FiH", "r", "h"), ("FiH", "r", "l"), ("FiL", "r", "h")]
    for which, terms in ((0, terms1), (1, terms2)):
        for mc in range(NCH):
            gi = which * NCH + mc
            s_mm.bump()
            def fmm(e, which=which, terms=terms, mc=mc, gi=gi):
                if gi == 0:
                    e.wait_ge(s_spl.sem, 1)
                if gi >= 2:
                    e.wait_ge(s_ev.sem, gi - 1)
                pb = pm[gi % 2]
                first = True
                inst = None
                nterm = len(terms) * NCH
                k = 0
                for (fn_, wn, hl) in terms:
                    Wt = Wh[wn] if hl == "h" else Wl[wn]
                    for c in range(NCH):
                        k += 1
                        inst = e.matmul(pb[:],
                                        Fs[fn_][:].rearrange("p (c m) -> p c m", m=D)[:, c, mc * 128:(mc + 1) * 128],
                                        Wt[:].rearrange("p (c m) -> p c m", m=128)[:, c, :],
                                        start=first, stop=(k == nterm))
                        first = False
                inst.then_inc(s_mm.sem, 1)
            p.on("tensor", fmm)
            s_ev.bump()
            def fev(e, which=which, mc=mc, gi=gi):
                e.wait_ge(s_mm.sem, gi + 1)
                e.tensor_copy(out=Co[which][:].rearrange("p (c m) -> p c m", m=128)[:, mc, :],
                              in_=pm[gi % 2][:]).then_inc(s_ev.sem, 1)
            p.on("vector", fev)
    for which, ap in ((0, C1o), (1, C2o)):
        s_st.bump(16)
        def fst(e, which=which, ap=ap):
            e.wait_ge(s_ev.sem, s_ev.v)
            e.dma_start(out=ap.rearrange("(c p) m -> p c m", p=128),
                        in_=Co[which][:].rearrange("p (c m) -> p c m", m=128)).then_inc(s_st.sem, 16)
        p.on("sync", fst)
    p.on("sync", lambda e: e.wait_ge(s_st.sem, s_st.v))
    p.build()
    ctx.close()
    return nc


def build_prog_m_sliced():
    nc = bass.Bass("TRN2", target_bir_lowering=False)
    ctx = ExitStack()
    C1 = nc.dram_tensor("C1", [D, D], F32, kind="ExternalInput").ap()
    C2 = nc.dram_tensor("C2", [D, D], F32, kind="ExternalInput").ap()
    GrHs = nc.dram_tensor("GrHs", [D, 128], BF16, kind="ExternalInput").ap()
    GrLs = nc.dram_tensor("GrLs", [D, 128], BF16, kind="ExternalInput").ap()
    GinHs = nc.dram_tensor("GinHs", [D, 128], BF16, kind="ExternalInput").ap()
    GinLs = nc.dram_tensor("GinLs", [D, 128], BF16, kind="ExternalInput").ap()
    Mo = nc.dram_tensor("Mo", [D, 128], F32, kind="ExternalOutput").ap()

    p = Prog(nc, ctx)
    s_ld = p.sem("ld")
    s_spl = p.sem("spl")
    s_mm = p.sem("mm")
    s_ev = p.sem("ev")
    s_st = p.sem("st")

    Cf = [sbt(nc, ctx, f"mC{k}", [128, NCH * D], F32) for k in range(2)]
    for k, ap in ((0, C1), (1, C2)):
        s_ld.bump(16)
        def fn(e, k=k, ap=ap):
            e.dma_start(out=Cf[k][:].rearrange("p (c m) -> p c m", m=D),
                        in_=ap.rearrange("(c p) m -> p c m", p=128)).then_inc(s_ld.sem, 16)
        p.on("sync", fn)
    Gs = {}
    for n, ap in (("GrHs", GrHs), ("GrLs", GrLs), ("GinHs", GinHs), ("GinLs", GinLs)):
        Gs[n] = sbt(nc, ctx, "s" + n, [128, NCH * 128], BF16)
        s_ld.bump(16)
        def fn(e, n=n, ap=ap):
            e.dma_start(out=Gs[n][:].rearrange("p (c m) -> p c m", m=128),
                        in_=ap.rearrange("(c p) m -> p c m", p=128)).then_inc(s_ld.sem, 16)
        p.on("sync", fn)

    CTh = [sbt(nc, ctx, f"mCTh{k}", [128, NCH * D], BF16) for k in range(2)]
    CTl = [sbt(nc, ctx, f"mCTl{k}", [128, NCH * D], BF16) for k in range(2)]
    ptr = [pst(nc, ctx, f"mptr{k}", [128, 128]) for k in range(2)]
    identf = sbt(nc, ctx, "midentf", [128, 128], F32)
    idf_hbm = nc.dram_tensor("identf", [128, 128], F32, kind="ExternalInput").ap()
    s_ld.bump(16)
    p.on("sync", lambda e: e.dma_start(out=identf[:], in_=idf_hbm).then_inc(s_ld.sem, 16))
    s_tr = p.sem("tr")
    s_trev = p.sem("trev")
    tscr = sbt(nc, ctx, "mtscr", [128, 128], F32)
    blocks = [(q, ic, kc) for q in range(2) for ic in range(NCH) for kc in range(NCH)]
    for bi, (q, ic, kc) in enumerate(blocks):
        s_tr.bump()
        def ftr(e, q=q, ic=ic, kc=kc, bi=bi):
            if bi == 0:
                e.wait_ge(s_ld.sem, s_ld.v)
            if bi >= 2:
                e.wait_ge(s_trev.sem, 2 * (bi - 1))
            e.transpose(ptr[bi % 2][:],
                        Cf[q][:].rearrange("p (c m) -> p c m", m=D)[:, ic, kc * 128:(kc + 1) * 128],
                        identf[:]).then_inc(s_tr.sem, 1)
        p.on("tensor", ftr)
        s_trev.bump(2)
        def fte(e, q=q, ic=ic, kc=kc, bi=bi):
            e.wait_ge(s_tr.sem, bi + 1)
            dsth = CTh[q][:].rearrange("p (c m) -> p c m", m=D)[:, kc, ic * 128:(ic + 1) * 128]
            dstl = CTl[q][:].rearrange("p (c m) -> p c m", m=D)[:, kc, ic * 128:(ic + 1) * 128]
            p.ss("vector", e, e.tensor_copy(out=dsth, in_=ptr[bi % 2][:]))
            p.ss("vector", e, e.tensor_copy(out=tscr[:], in_=dsth))
            p.ss("vector", e, e.tensor_sub(out=tscr[:], in0=ptr[bi % 2][:], in1=tscr[:]))
            e.tensor_copy(out=dstl, in_=tscr[:]).then_inc(s_trev.sem, 2)
        p.on("vector", fte)

    pm = [pst(nc, ctx, f"mpm{k}", [128, 128]) for k in range(2)]
    Msb = sbt(nc, ctx, "sM", [128, NCH * 128], F32)
    s_mm2 = p.sem("mm2")
    s_ev2 = p.sem("ev2")
    terms = [(0, "GrHs", "h"), (0, "GrLs", "h"), (0, "GrHs", "l"),
             (1, "GinHs", "h"), (1, "GinLs", "h"), (1, "GinHs", "l")]
    for ic in range(NCH):
        s_mm2.bump()
        def fmm(e, ic=ic):
            if ic == 0:
                e.wait_ge(s_trev.sem, s_trev.v)
            if ic >= 2:
                e.wait_ge(s_ev2.sem, ic - 1)
            pb = pm[ic % 2]
            first = True
            inst = None
            nterm = len(terms) * NCH
            k = 0
            for (q, gn, hl) in terms:
                CT = CTh[q] if hl == "h" else CTl[q]
                for kc in range(NCH):
                    k += 1
                    inst = e.matmul(pb[:],
                                    CT[:].rearrange("p (c m) -> p c m", m=D)[:, kc, ic * 128:(ic + 1) * 128],
                                    Gs[gn][:].rearrange("p (c m) -> p c m", m=128)[:, kc, :],
                                    start=first, stop=(k == nterm))
                    first = False
            inst.then_inc(s_mm2.sem, 1)
        p.on("tensor", fmm)
        s_ev2.bump()
        def fev(e, ic=ic):
            e.wait_ge(s_mm2.sem, ic + 1)
            e.tensor_copy(out=Msb[:].rearrange("p (c m) -> p c m", m=128)[:, ic, :],
                          in_=pm[ic % 2][:]).then_inc(s_ev2.sem, 1)
        p.on("vector", fev)
    s_st2 = p.sem("st2")
    s_st2.bump(16)
    def fst(e):
        e.wait_ge(s_ev2.sem, s_ev2.v)
        e.dma_start(out=Mo.rearrange("(c p) m -> p c m", p=128),
                    in_=Msb[:].rearrange("p (c m) -> p c m", m=128)).then_inc(s_st2.sem, 16)
    p.on("sync", fst)
    p.on("sync", lambda e: e.wait_ge(s_st2.sem, s_st2.v))
    p.build()
    ctx.close()
    return nc



L2 = lambda fn: fn


def emit_ffn0(p, nc, ctx, x_in, M, bias, identb, onesb, x_out, stage=2, dbg=None):
    s_ld = p.gsem("ld")
    s_gld = p.gsem("gld")
    s_st = p.gsem("st")

    x_sb = sbt(nc, ctx, "x_sb", [128, NT * D], F32)
    ident_b = sbt(nc, ctx, "identb_sb", [128, 128], BF16)
    ones_b = sbt(nc, ctx, "ones_sb", [1, 128], BF16)
    bias_b = sbt(nc, ctx, "bias_sb", [1, D], BF16)

    xv = emit_load_x(p, x_in, x_sb, s_ld)
    for t, src in ((ident_b, identb), (ones_b, onesb)):
        s_ld.bump(16)
        def fn(e, t=t, src=src):
            e.dma_start(out=t[:], in_=src).then_inc(s_ld.sem, 16)
        p.on("sync", fn)
    s_gld.bump(16)
    p.on("gpsimd", lambda e: e.dma_start(out=bias_b[:], in_=bias).then_inc(s_gld.sem, 16))
    M_sb = emit_weight_load(p, nc, ctx, "M_sb", M, s_gld)
    ld_all = s_ld.v
    gld_all = s_gld.v

    if stage == 0:
        emit_store_x(p, x_sb, x_out, s_st, s_ld, ld_all)
        return (s_st, s_st.v)
    else:
        st = emit_ln_stats(p, nc, ctx, "ln", x_sb, s_ld, ld_all)
        xn_ring = sbt(nc, ctx, "xn_ring", [128, 2 * D], BF16)
        xnT_ring = sbt(nc, ctx, "xnT_ring", [128, NCH * 2 * 128], BF16)
        xnTv = xnT_ring[:].rearrange("p (c t) -> p c t", t=256)
        ptr_pair = [pst(nc, ctx, f"ptr{k}", [128, 512], BF16) for k in range(2)]
        pm_pair = [pst(nc, ctx, f"pm{k}", [128, 512]) for k in range(2)]
        s_apply = p.sem(); s_tr = p.sem(); s_trev = p.sem(); s_mm = p.sem(); s_done = p.sem()
        wv = M_sb[:].rearrange("p (c m) -> p c m", m=D)
        xv = x_sb[:].rearrange("p (i c) -> p i c", c=D)
        for i in range(NT):
            slot = i % 2
            pw = None
            if i >= 2:
                tv = 2 * (i - 2) + 2 + s_tr.base
                pw = (lambda e, tv=tv: e.wait_ge(s_tr.sem, tv))
            emit_ln_apply_tile(p, st, x_sb, i, xn_ring[:, slot * D:(slot + 1) * D],
                               s_apply, prior_wait=pw)
            ww = None
            if i >= 2:
                mv = 2 * (i - 2) + 2 + s_mm.base
                ww = (lambda e, mv=mv: e.wait_ge(s_mm.sem, mv))
            emit_transpose_tile(p, xn_ring[:, slot * D:(slot + 1) * D], ident_b,
                                ptr_pair, xnTv, slot, s_apply, s_apply.v, s_tr, s_trev,
                                war_wait=ww)
            for h in range(2):
                gi = 2 * i + h
                s_mm.bump()
                def fmm(e, i=i, h=h, gi=gi, slot=slot, tev=s_trev.v + s_trev.base,
                        lda=ld_all + s_ld.base, glda=gld_all, gda=gi - 1 + s_done.base):
                    if gi == 0:
                        e.wait_ge(s_ld.sem, lda)
                        e.wait_ge(s_gld.sem, glda)
                    if h == 0:
                        e.wait_ge(s_trev.sem, tev)
                    if gi >= 2:
                        e.wait_ge(s_done.sem, gda)
                    pb = pm_pair[gi % 2]
                    for c in range(NCH):
                        e.matmul(pb[:], xnTv[:, c, slot * 128:(slot + 1) * 128],
                                 wv[:, c, h * 512:(h + 1) * 512],
                                 start=(c == 0), stop=False)
                    e.matmul(pb[:], ones_b[0:1, :], bias_b[0:1, h * 512:(h + 1) * 512],
                             start=False, stop=True).then_inc(s_mm.sem, 1)
                p.on("tensor", fmm)
                s_done.bump()
                def fev(e, i=i, h=h, gi=gi, gma=gi + 1 + s_mm.base):
                    e.wait_ge(s_mm.sem, gma)
                    e.scalar_tensor_tensor(out=xv[:, i, h * 512:(h + 1) * 512],
                                           in0=pm_pair[gi % 2][:], scalar=1.0,
                                           in1=xv[:, i, h * 512:(h + 1) * 512],
                                           op0=ALU.mult, op1=ALU.add).then_inc(s_done.sem, 1)
                p.on("vector", fev)
        if dbg is not None:
            dscr = sbt(nc, ctx, "dbg_scr", [128, 96], F32)
            def fdbg(e):
                e.tensor_copy(out=dscr[:, 0:NT], in_=st["mu"][:])
                e.tensor_copy(out=dscr[:, NT:2 * NT], in_=st["rstd"][:])
                return e.tensor_copy(out=dscr[:, 2 * NT:3 * NT], in_=st["nmu"][:])
            p.op("vector", fdbg, waits=[(s_done, s_done.v)], inc=s_done)
            p.op("sync", L2(lambda e: e.dma_start(out=dbg, in_=dscr[:])),
                 waits=[(s_done, s_done.v)], inc=s_st, inc_n=16)
        emit_store_x(p, x_sb, x_out, s_st, s_done, s_done.v)
        return (s_st, s_st.v)


def build_prog_ffn0(stage=2):
    nc = bass.Bass("TRN2", target_bir_lowering=False)
    ctx = ExitStack()
    x_in = nc.dram_tensor("x_in", [T, D], F32, kind="ExternalInput").ap()
    M = nc.dram_tensor("M", [D, D], F32, kind="ExternalInput").ap()
    bias = nc.dram_tensor("bias", [1, D], F32, kind="ExternalInput").ap()
    identb = nc.dram_tensor("identb", [128, 128], BF16, kind="ExternalInput").ap()
    onesb = nc.dram_tensor("onesb", [1, 128], BF16, kind="ExternalInput").ap()
    x_out = nc.dram_tensor("x_out", [T, D], F32, kind="ExternalOutput").ap()

    p = Prog(nc, ctx)
    emit_ffn0(p, nc, ctx, x_in=x_in, M=M, bias=bias, identb=identb, onesb=onesb, x_out=x_out, stage=stage)
    p.build()
    ctx.close()
    return nc






def emit_attn_a(p, nc, ctx, x_in, w_qkv, b_qkv, w_proj, b_proj, identb, onesb, amasks, x_out):
    s_ld = p.gsem("ld")

    ident_b = sbt(nc, ctx, "identb_sb", [128, 128], BF16)
    ones_b = sbt(nc, ctx, "ones_sb", [1, 128], BF16)
    ones64 = sbt(nc, ctx, "ones64_sb", [128, 64], F32)
    masks = sbt(nc, ctx, "masks_sb", [128, 4 * 512], BF16)
    wq_sb = sbt(nc, ctx, "wq_sb", [128, NCH * 1536], BF16)
    wp_sb = sbt(nc, ctx, "wp_sb", [128, NCH * D], BF16)
    bqk_sb = sbt(nc, ctx, "bqk_sb", [128, 16], F32)
    bv_sb = sbt(nc, ctx, "bv_sb", [1, D], BF16)
    bp_sb = sbt(nc, ctx, "bp_sb", [1, D], BF16)
    cneg = sbt(nc, ctx, "cneg", [128, 1], F32)
    cscale = sbt(nc, ctx, "cscale", [128, 1], F32)

    qT = sbt(nc, ctx, "qT", [128, 4 * T], BF16)
    kT = sbt(nc, ctx, "kT", [128, 4 * T], BF16)
    vA = sbt(nc, ctx, "vA", [128, NT * 8 * 65], BF16)
    xt_ring = sbt(nc, ctx, "xt_ring", [128, 4 * D], F32)
    xn_ring = sbt(nc, ctx, "xn_ring", [128, 4 * D], BF16)
    xnT_ring = sbt(nc, ctx, "xnT_ring", [128, NCH * 512], BF16)
    pt_sb = sbt(nc, ctx, "pt_sb", [128, 2 * 512], BF16)
    yT = sbt(nc, ctx, "yT", [128, 4 * 512], BF16)
    rc_sb = sbt(nc, ctx, "rc_sb", [128, 512], F32)
    xs_ring = sbt(nc, ctx, "xs_ring", [128, 4 * D], F32)

    ptrp = [pst(nc, ctx, f"ptr{k}", [128, 512], BF16) for k in range(2)]
    pqk = [pst(nc, ctx, f"pqk{k}", [128, 512]) for k in range(2)]
    pav = [pst(nc, ctx, f"pav{k}", [65, 512]) for k in range(2)]
    ppr = [pst(nc, ctx, f"ppr{k}", [128, 512]) for k in range(2)]

    L = lambda fn: fn
    wqv = wq_sb[:].rearrange("p (c m) -> p c m", m=1536)
    wpv = wp_sb[:].rearrange("p (c m) -> p c m", m=D)

    p.op("gpsimd", lambda e: e.dma_start(out=ident_b[:], in_=identb), inc=s_ld, inc_n=16)
    p.op("gpsimd", lambda e: e.dma_start(out=ones_b[:], in_=onesb), inc=s_ld, inc_n=16)
    p.op("gpsimd", lambda e: e.dma_start(
        out=masks[:].rearrange("p (r f) -> p r f", f=512),
        in_=amasks.rearrange("(r p) f -> p r f", p=128)), inc=s_ld, inc_n=16)
    p.op("gpsimd", lambda e: e.dma_start(
        out=wpv, in_=w_proj.rearrange("(c p) m -> p c m", p=128)), inc=s_ld, inc_n=16)
    p.op("gpsimd", lambda e: e.dma_start(
        out=bqk_sb[:, 0:8], in_=b_qkv[0:1, 0:1024].rearrange("o (c p) -> p (o c)", p=128),
        allow_slow_non_contiguous=True), inc=s_ld, inc_n=16)
    p.op("gpsimd", lambda e: e.dma_start(
        out=bqk_sb[:, 8:16], in_=b_qkv[0:1, 1024:2048].rearrange("o (c p) -> p (o c)", p=128),
        allow_slow_non_contiguous=True), inc=s_ld, inc_n=16)
    p.op("gpsimd", lambda e: e.dma_start(out=bv_sb[:], in_=b_qkv[0:1, 2048:3072]),
         inc=s_ld, inc_n=16)
    p.op("gpsimd", lambda e: e.dma_start(out=bp_sb[:], in_=b_proj), inc=s_ld, inc_n=16)
    ld_all = s_ld.v

    xsrc = x_in.rearrange("(i p) c -> p i c", p=128)
    xdst = x_out.rearrange("(i p) c -> p i c", p=128)
    xtv = xt_ring[:].rearrange("p (i c) -> p i c", c=D)
    xnv = xn_ring[:].rearrange("p (i c) -> p i c", c=D)
    xnTv = xnT_ring[:].rearrange("p (c t) -> p c t", t=512)
    qTv = qT[:].rearrange("p (c t) -> p c t", t=T)
    kTv = kT[:].rearrange("p (c t) -> p c t", t=T)
    vAv = vA[:].rearrange("p (i h f) -> p i h f", h=8, f=65)
    mskv = masks[:].rearrange("p (r f) -> p r f", f=512)
    ptv = pt_sb[:].rearrange("p (s f) -> p s f", f=512)
    yTv = yT[:].rearrange("p (c f) -> p c f", f=512)
    xsv = xs_ring[:].rearrange("p (i c) -> p i c", c=D)

    ssum = sbt(nc, ctx, "st_sum", [128, NT])
    sssq = sbt(nc, ctx, "st_ssq", [128, NT])
    smu = sbt(nc, ctx, "st_mu", [128, NT])
    srstd = sbt(nc, ctx, "st_rstd", [128, NT])
    snmu = sbt(nc, ctx, "st_nmu", [128, NT])
    scr_bf = sbt(nc, ctx, "st_scr", [128, D], BF16)
    s_xld = p.sem("xld")
    s_stat = p.sem("stat")
    s_post = p.sem("post")
    s_sqrt = p.sem("sqrt")
    for i in range(NT):
        w = [(s_stat, i - 3)] if i >= 4 else []
        xv_ = p.op("sync", L(lambda e, i=i: e.dma_start(out=xtv[:, i % 4, :], in_=xsrc[:, i, :])),
                   waits=w, inc=s_xld, inc_n=16)
        p.op("scalar", L(lambda e, i=i: (
            e.activation(out=scr_bf[:], in_=xtv[:, i % 4, :], func=AF.Copy,
                         accum_out=ssum[:, i:i + 1]),
            e.activation(out=scr_bf[:], in_=xtv[:, i % 4, :], func=AF.Square,
                         accum_out=sssq[:, i:i + 1]))[-1]),
            waits=[(s_xld, xv_)], inc=s_stat)

    def post1(e):
        p.ss("vector", e, e.memset(cneg[:], -24.0))
        p.ss("vector", e, e.memset(cscale[:], 0.125))
        p.ss("vector", e, e.memset(ones64[:], 1.0))
        p.ss("vector", e, e.memset(vAv[:, :, :, 64:65], 1.0))
        p.ss("vector", e, e.tensor_scalar(out=smu[:], in0=ssum[:], scalar1=1.0 / D, scalar2=None, op0=ALU.mult))
        p.ss("vector", e, e.tensor_tensor(out=srstd[:], in0=smu[:], in1=smu[:], op=ALU.mult))
        p.ss("vector", e, e.tensor_scalar(out=srstd[:], in0=srstd[:], scalar1=-1e-5, scalar2=None, op0=ALU.add))
        return e.scalar_tensor_tensor(out=srstd[:], in0=sssq[:], scalar=1.0 / D,
                                      in1=srstd[:], op0=ALU.mult, op1=ALU.subtract)
    p.op("vector", post1, waits=[(s_stat, NT), (s_ld, ld_all)], inc=s_post)
    p.op("scalar", lambda e: e.activation(out=srstd[:], in_=srstd[:], func=AF.Sqrt),
         waits=[(s_post, 1)], inc=s_sqrt)

    def post2(e):
        p.ss("vector", e, e.reciprocal(out=srstd[:], in_=srstd[:]))
        p.ss("vector", e, e.tensor_tensor(out=snmu[:], in0=smu[:], in1=srstd[:], op=ALU.mult))
        return e.tensor_scalar(out=snmu[:], in0=snmu[:], scalar1=-1.0, scalar2=None, op0=ALU.mult)
    p.op("vector", post2, waits=[(s_sqrt, 1)], inc=s_post)

    s_xld2 = p.sem("xld2")
    s_apply = p.sem("apply")
    s_tr = p.sem("tr")
    s_trev = p.sem("trev")
    s_v = p.sem("v")
    s_vev = p.sem("vev")
    s_q = p.sem("q")
    s_qev = p.sem("qev")
    s_qk = p.sem("qk")
    s_exp = p.sem("exp")
    s_msk = p.sem("msk")
    s_av = p.sem("av")
    s_ye = p.sem("ye")
    s_rc = p.sem("rc")
    s_rpl = p.sem("rpl")
    s_nrm = p.sem("nrm")
    s_pr = p.sem("pr")
    s_pre = p.sem("pre")
    s_xs = p.sem("xs")
    s_sto = p.sem("sto")
    s_stog = p.gsem("stog")
    s_fin = p.sem("fin")

    s_wl = p.sem("wl")
    for half in range(2):
        half_gate = [(s_pre, s_pre.v), (s_sto, s_sto.v)] if half == 1 else []
        wsrc = w_qkv.rearrange("(c p) m -> p c m", p=128)
        for sl, (m0, mw) in enumerate(((half * 512, 512), (D + half * 512, 512),
                                       (2048 + half * 512, 512))):
            wwar = [(s_v, s_v.v), (s_q, s_q.v)] if half == 1 else []
            p.op("gpsimd", L(lambda e, sl=sl, m0=m0: e.dma_start(
                out=wqv[:, :, sl * 512:(sl + 1) * 512], in_=wsrc[:, :, m0:m0 + 512])),
                waits=wwar, inc=s_wl, inc_n=16)
        for i in range(NT):
            g = half * NT + i
            w = [(s_apply, g - 3)] if g >= 4 else []
            xv_ = p.op("scalar", L(lambda e, i=i: e.dma_start(out=xtv[:, i % 4, :], in_=xsrc[:, i, :])),
                       waits=w, inc=s_xld2, inc_n=16)
            w = [(s_xld2, xv_), (s_post, 2)]
            if g >= 4:
                w.append((s_tr, 2 * (g - 4) + 2))
            p.op("vector", L(lambda e, i=i: e.tensor_scalar(
                out=xnv[:, i % 4, :], in0=xtv[:, i % 4, :],
                scalar1=srstd[:, i:i + 1], scalar2=snmu[:, i:i + 1],
                op0=ALU.mult, op1=ALU.add)), waits=w, inc=s_apply)
            for c4 in (0, 4):
                par = s_tr.v % 2
                w = [(s_apply, s_apply.v)]
                if s_trev.v >= 2:
                    w.append((s_trev, s_trev.v - 1))
                def ftr(e, i=i, c4=c4, par=par):
                    inst = None
                    for j in range(4):
                        inst = e.transpose(ptrp[par][:, j * 128:(j + 1) * 128],
                                           xnv[:, i % 4, (c4 + j) * 128:(c4 + j + 1) * 128],
                                           ident_b[:])
                    return inst
                p.op("tensor", ftr, waits=w, inc=s_tr)
                w = [(s_tr, s_tr.v)] + half_gate
                if g >= 4 and c4 == 0:
                    sp_done = (g // 4)
                    w.append((s_v, 4 * sp_done))
                    w.append((s_q, 8 * sp_done))
                p.op("scalar", L(lambda e, i=i, c4=c4, par=par: e.activation(
                    out=xnTv[:, c4:c4 + 4, (i % 4) * 128:(i % 4 + 1) * 128],
                    in_=ptrp[par][:].rearrange("p (c t) -> p c t", t=128),
                    func=AF.Copy)), waits=w, inc=s_trev)
            par = s_v.v % 2
            def fvmm(e, i=i, half=half, par=par):
                for c in range(NCH):
                    e.matmul(ppr[par][:], xnTv[:, c, (i % 4) * 128:(i % 4 + 1) * 128],
                             wqv[:, c, 1024:1536],
                             start=(c == 0), stop=False)
                return e.matmul(ppr[par][:], ones_b[0:1, :],
                                bv_sb[0:1, half * 512:(half + 1) * 512],
                                start=False, stop=True)
            w = [(s_trev, s_trev.v), (s_wl, s_wl.v)]
            if s_v.v >= 2:
                w.append((s_vev, s_vev.v - 1))
            p.op("tensor", fvmm, waits=w, inc=s_v)
            p.op("vector", L(lambda e, i=i, par=par: e.tensor_copy(
                out=vAv[:, i, :, 0:64],
                in_=ppr[par][:].rearrange("p (h f) -> p h f", f=64))),
                waits=[(s_v, s_v.v)], inc=s_vev)
            if i % 4 == 3:
                span = i // 4
                for which, dstv, col0, bofs in (("q", qTv, 0, half * 4),
                                                ("k", kTv, 512, 8 + half * 4)):
                    for mc in range(4):
                        par = s_q.v % 2
                        def fq(e, mc=mc, col0=col0, par=par):
                            inst = None
                            for c in range(NCH):
                                inst = e.matmul(pqk[par][:],
                                                wqv[:, c, col0 + mc * 128: col0 + (mc + 1) * 128],
                                                xnTv[:, c, :],
                                                start=(c == 0), stop=(c == NCH - 1))
                            return inst
                        w = [(s_trev, s_trev.v)]
                        if s_q.v >= 2:
                            w.append((s_qev, s_qev.v - 1))
                        p.op("tensor", fq, waits=w, inc=s_q)
                        p.op("scalar", L(lambda e, span=span, mc=mc, dstv=dstv, bofs=bofs, par=par:
                                         e.activation(out=dstv[:, mc, span * 512:(span + 1) * 512],
                                                      in_=pqk[par][:], func=AF.Identity,
                                                      bias=bqk_sb[:, bofs + mc:bofs + mc + 1])),
                             waits=[(s_q, s_q.v)], inc=s_qev)
        qkv_done = [(s_qev, s_qev.v), (s_vev, s_vev.v)]
        for s in range(8):
            qs = 512 * s
            if half == 0:
                for ti in range(4):
                    gt = 4 * s + ti
                    w = list(qkv_done) if gt == 0 else []
                    if gt >= 4:
                        w.append((s_sto, 16 * (gt - 3)))
                    p.op("sync", L(lambda e, gt=gt: e.dma_start(
                        out=xsv[:, gt % 4, :], in_=xsrc[:, gt, :])),
                        waits=w, inc=s_xs, inc_n=16)
            nkt = 4 * s + 4
            for pp in range(4):
                heads = [2 * pp, 2 * pp + 1]
                for kt in range(nkt):
                    for j in (0, 1):
                        hloc = heads[j]
                        rb = (hloc % 2) * 64
                        ch = hloc // 2
                        w = list(qkv_done)
                        if s_exp.v > 1:
                            w.append((s_exp, s_exp.v - 1))
                        p.op("tensor", L(lambda e, rb=rb, ch=ch, kt=kt, j=j, qs=qs:
                                         e.matmul(pqk[j][:],
                                                  kTv[rb:rb + 64, ch, kt * 128:(kt + 1) * 128],
                                                  qTv[rb:rb + 64, ch, qs:qs + 512],
                                                  start=True, stop=True,
                                                  tile_position=(rb, 0))),
                             waits=w, inc=s_qk)
                    masked = kt >= 4 * s
                    for j in (0, 1):
                        hloc = heads[j]
                        w = [(s_qk, s_qk.v - 1 + j)]
                        if s_av.v > 1:
                            w.append((s_av, s_av.v - 1))
                        p.op("scalar", L(lambda e, j=j: e.activation(
                            out=ptv[:, j, :], in_=pqk[j][:], func=AF.Exp,
                            bias=cneg[:], scale=cscale[:])),
                            waits=w, inc=s_exp)
                        if masked:
                            p.op("vector", L(lambda e, j=j, r=kt - 4 * s: e.tensor_mul(
                                out=ptv[:, j, :], in0=ptv[:, j, :], in1=mskv[:, r, :])),
                                waits=[(s_exp, s_exp.v)], inc=s_msk)
                        w = [(s_msk, s_msk.v)] if masked else [(s_exp, s_exp.v)]
                        if kt == 0:
                            w += [(s_ye, s_ye.v), (s_rc, s_rc.v)]
                        p.op("tensor", L(lambda e, j=j, kt=kt, hloc=hloc,
                                         first=(kt == 0), last=(kt == nkt - 1):
                                         e.matmul(pav[j][:], vAv[:, kt, hloc, :], ptv[:, j, :],
                                                  start=first, stop=last)),
                             waits=w, inc=s_av)
                for j in (0, 1):
                    hloc = heads[j]
                    rb = (hloc % 2) * 64
                    ch = hloc // 2
                    p.op("scalar", L(lambda e, rb=rb, ch=ch, j=j: e.activation(
                        out=yTv[rb:rb + 64, ch, :], in_=pav[j][0:64, :], func=AF.Copy)),
                        waits=[(s_av, s_av.v - 1 + j), (s_pr, s_pr.v)], inc=s_ye)
                    p.op("vector", L(lambda e, j=j: e.reciprocal(
                        out=rc_sb[64:65, :], in_=pav[j][64:65, :])),
                        waits=[(s_av, s_av.v - 1 + j), (s_rpl, s_rpl.v)], inc=s_rc)
                    par = s_rpl.v % 2
                    w = [(s_rc, s_rc.v)]
                    if s_rpl.v >= 2:
                        w.append((s_nrm, s_nrm.v - 1))
                    p.op("tensor", L(lambda e, par=par: e.matmul(
                        ppr[par][0:64, :], ones64[64:65, :], rc_sb[64:65, :],
                        start=True, stop=True, tile_position=(64, 0))),
                        waits=w, inc=s_rpl)
                    p.op("vector", L(lambda e, rb=rb, ch=ch, par=par: e.tensor_mul(
                        out=yTv[rb:rb + 64, ch, :],
                        in0=yTv[rb:rb + 64, ch, :],
                        in1=ppr[par][0:64, :])),
                        waits=[(s_rpl, s_rpl.v), (s_ye, s_ye.v)], inc=s_nrm)
            for ti in range(4):
                for mh in range(2):
                    par = s_pr.v % 2
                    def fpr(e, ti=ti, mh=mh, half=half, par=par):
                        inst = None
                        for c in range(4):
                            last = (c == 3 and half == 1)
                            inst = e.matmul(ppr[par][:], yTv[:, c, ti * 128:(ti + 1) * 128],
                                            wpv[:, half * 4 + c, mh * 512:(mh + 1) * 512],
                                            start=(c == 0), stop=last)
                        if half == 0:
                            inst = e.matmul(ppr[par][:], ones_b[0:1, :],
                                            bp_sb[0:1, mh * 512:(mh + 1) * 512],
                                            start=False, stop=True)
                        return inst
                    w = [(s_nrm, s_nrm.v)]
                    if s_pr.v >= 2:
                        w.append((s_pre, s_pre.v - 1))
                    p.op("tensor", fpr, waits=w, inc=s_pr)
                    gt = 4 * s + ti
                    if half == 0:
                        p.op("vector", L(lambda e, gt=gt, mh=mh, par=par: e.scalar_tensor_tensor(
                            out=xsv[:, gt % 4, mh * 512:(mh + 1) * 512],
                            in0=ppr[par][:], scalar=1.0,
                            in1=xsv[:, gt % 4, mh * 512:(mh + 1) * 512],
                            op0=ALU.mult, op1=ALU.add)),
                            waits=[(s_pr, s_pr.v), (s_xs, 16 * (gt + 1))], inc=s_pre)
                    else:
                        w = [(s_pr, s_pr.v)]
                        if gt >= 4:
                            w.append((s_stog, 16 * (gt - 3)))
                        p.op("vector", L(lambda e, gt=gt, mh=mh, par=par: e.tensor_copy(
                            out=xsv[:, gt % 4, mh * 512:(mh + 1) * 512],
                            in_=ppr[par][:])),
                            waits=w, inc=s_pre)
                if half == 0:
                    p.op("sync", L(lambda e, gt=4 * s + ti: e.dma_start(
                        out=xdst[:, gt, :], in_=xsv[:, gt % 4, :])),
                        waits=[(s_pre, s_pre.v)], inc=s_sto, inc_n=16)
                else:
                    p.op("gpsimd", L(lambda e, gt=4 * s + ti: e.dma_start(
                        out=xdst[:, gt, :], in_=xsv[:, gt % 4, :],
                        accum_op=ALU.add)),
                        waits=[(s_pre, s_pre.v)], inc=s_stog, inc_n=16)
    p.op("sync", lambda e: e.nop(), waits=[(s_sto, s_sto.v), (s_stog, s_stog.v)],
         inc=s_fin)
    return (s_fin, s_fin.v)


def build_prog_attn_a():
    nc = bass.Bass("TRN2", target_bir_lowering=False)
    ctx = ExitStack()
    x_in = nc.dram_tensor("x_in", [T, D], F32, kind="ExternalInput").ap()
    w_qkv = nc.dram_tensor("w_qkv", [D, 3 * D], F32, kind="ExternalInput").ap()
    b_qkv = nc.dram_tensor("b_qkv", [1, 3 * D], F32, kind="ExternalInput").ap()
    w_proj = nc.dram_tensor("w_proj", [D, D], F32, kind="ExternalInput").ap()
    b_proj = nc.dram_tensor("b_proj", [1, D], F32, kind="ExternalInput").ap()
    identb = nc.dram_tensor("identb", [128, 128], BF16, kind="ExternalInput").ap()
    onesb = nc.dram_tensor("onesb", [1, 128], BF16, kind="ExternalInput").ap()
    amasks = nc.dram_tensor("amasks", [4 * 128, 512], BF16, kind="ExternalInput").ap()
    x_out = nc.dram_tensor("x_out", [T, D], F32, kind="ExternalOutput").ap()

    p = Prog(nc, ctx)
    emit_attn_a(p, nc, ctx, x_in=x_in, w_qkv=w_qkv, b_qkv=b_qkv, w_proj=w_proj, b_proj=b_proj, identb=identb, onesb=onesb, amasks=amasks, x_out=x_out)
    p.build()
    ctx.close()
    return nc



def _qb_chunk(h):
    return h % 4 + 4 * (h // 8)


def _qb_row(h):
    return 64 * ((h // 4) % 2)


def emit_attn_b(p, nc, ctx, x_in, Wq, Wk, Wv, Wo, identb, bmasks, ctok, stok, x_out, dbg=None):
    L = lambda fn: fn
    s_ld = p.gsem("ld")

    ident_b = sbt(nc, ctx, "identb_sb", [128, 128], BF16)
    ones64 = sbt(nc, ctx, "ones64_sb", [128, 64], F32)
    cneg = sbt(nc, ctx, "cneg", [128, 1], F32)
    cscale = sbt(nc, ctx, "cscale", [128, 1], F32)
    masks = sbt(nc, ctx, "masks_sb", [128, 3 * 128], BF16)
    ct_sb = sbt(nc, ctx, "ct_sb", [128, NT * 32], F32)
    st_sb = sbt(nc, ctx, "stk_sb", [128, NT * 32], F32)
    wq_sb = sbt(nc, ctx, "wq_sb", [128, NCH * D], BF16)
    wkv_sb = sbt(nc, ctx, "wkv_sb", [128, NCH * 512], BF16)
    wo_sb = sbt(nc, ctx, "wo_sb", [128, NCH * D], BF16)

    qT = sbt(nc, ctx, "qT", [128, NCH * T], BF16)
    kT = sbt(nc, ctx, "kT", [128, 2 * T], BF16)
    vA = sbt(nc, ctx, "vA", [128, NT * 4 * 65], BF16)
    xt_ring = sbt(nc, ctx, "xt_ring", [128, 2 * D], F32)
    xn_ring = sbt(nc, ctx, "xn_ring", [128, 2 * D], BF16)
    xnT_ring = sbt(nc, ctx, "xnT_ring", [128, NCH * 256], BF16)
    qk_tok = sbt(nc, ctx, "qk_tok", [128, 2 * 1280], F32)
    qk_bf = sbt(nc, ctx, "qk_bf", [128, 2 * 1280], BF16)
    tmpr = sbt(nc, ctx, "tmpr", [128, 3 * 640], F32)
    rmsq = sbt(nc, ctx, "rmsq", [128, 2 * 20], F32)
    sq_scr = sbt(nc, ctx, "sq_scr", [128, 1280], F32)
    pt_sb = sbt(nc, ctx, "pt_sb", [128, 2 * 128], BF16)
    yT = sbt(nc, ctx, "yT", [128, NCH * 128], BF16)
    rc_sb = sbt(nc, ctx, "rc_sb", [128, 128], F32)
    xs_ring = sbt(nc, ctx, "xs_ring", [128, 2 * D], F32)

    ptrp = [pst(nc, ctx, f"ptr{k}", [128, 512], BF16) for k in range(2)]
    pq = [pst(nc, ctx, f"pq{k}", [128, 512]) for k in range(2)]
    pav = [pst(nc, ctx, f"pav{k}", [65, 128]) for k in range(2)]
    ppr = [pst(nc, ctx, f"ppr{k}", [128, 512]) for k in range(2)]

    p.op("gpsimd", lambda e: e.dma_start(out=ident_b[:], in_=identb), inc=s_ld, inc_n=16)
    p.op("gpsimd", lambda e: e.dma_start(
        out=masks[:].rearrange("p (r f) -> p r f", f=128),
        in_=bmasks.rearrange("(r p) f -> p r f", p=128)), inc=s_ld, inc_n=16)
    p.op("gpsimd", lambda e: e.dma_start(out=ct_sb[:], in_=ctok), inc=s_ld, inc_n=16)
    p.op("gpsimd", lambda e: e.dma_start(out=st_sb[:], in_=stok), inc=s_ld, inc_n=16)
    p.op("gpsimd", lambda e: e.dma_start(
        out=wq_sb[:].rearrange("p (c m) -> p c m", m=D),
        in_=Wq.rearrange("(c p) m -> p c m", p=128)), inc=s_ld, inc_n=16)
    p.op("gpsimd", lambda e: e.dma_start(
        out=wkv_sb[:].rearrange("p (c m) -> p c m", m=512)[:, :, 0:256],
        in_=Wk.rearrange("(c p) m -> p c m", p=128)), inc=s_ld, inc_n=16)
    p.op("gpsimd", lambda e: e.dma_start(
        out=wkv_sb[:].rearrange("p (c m) -> p c m", m=512)[:, :, 256:512],
        in_=Wv.rearrange("(c p) m -> p c m", p=128)), inc=s_ld, inc_n=16)
    p.op("gpsimd", lambda e: e.dma_start(
        out=wo_sb[:].rearrange("p (c m) -> p c m", m=D),
        in_=Wo.rearrange("(c p) m -> p c m", p=128)), inc=s_ld, inc_n=16)
    ld_all = s_ld.v

    xsrc = x_in.rearrange("(i p) c -> p i c", p=128)
    xdst = x_out.rearrange("(i p) c -> p i c", p=128)
    xtv = xt_ring[:].rearrange("p (i c) -> p i c", c=D)
    xnv = xn_ring[:].rearrange("p (i c) -> p i c", c=D)
    xnTv = xnT_ring[:].rearrange("p (c t) -> p c t", t=256)
    qTv = qT[:].rearrange("p (c t) -> p c t", t=T)
    kTv = kT[:].rearrange("p (c t) -> p c t", t=T)
    vAv = vA[:].rearrange("p (i h f) -> p i h f", h=4, f=65)
    mskv = masks[:].rearrange("p (r f) -> p r f", f=128)
    qkv_ = qk_tok[:].rearrange("p (s f) -> p s f", f=1280)
    qkb_ = qk_bf[:].rearrange("p (s f) -> p s f", f=1280)
    tmpv = tmpr[:].rearrange("p (k f) -> p k f", f=640)
    rmsv = rmsq[:].rearrange("p (s f) -> p s f", f=20)
    ctv = ct_sb[:].rearrange("p (i x) -> p i x", x=32)
    stv = st_sb[:].rearrange("p (i x) -> p i x", x=32)
    ptv = pt_sb[:].rearrange("p (s f) -> p s f", f=128)
    yTv = yT[:].rearrange("p (c f) -> p c f", f=128)
    xsv = xs_ring[:].rearrange("p (i c) -> p i c", c=D)

    ssum = sbt(nc, ctx, "st_sum", [128, NT])
    sssq = sbt(nc, ctx, "st_ssq", [128, NT])
    smu = sbt(nc, ctx, "st_mu", [128, NT])
    srstd = sbt(nc, ctx, "st_rstd", [128, NT])
    snmu = sbt(nc, ctx, "st_nmu", [128, NT])
    scr_bf = sbt(nc, ctx, "st_scr", [128, D], BF16)
    s_xld = p.sem("xld")
    s_stat = p.sem("stat")
    s_post = p.sem("post")
    s_sqrt = p.sem("sqrt")
    for i in range(NT):
        w = [(s_stat, i - 1)] if i >= 2 else []
        xv_ = p.op("sync", L(lambda e, i=i: e.dma_start(out=xtv[:, i % 2, :], in_=xsrc[:, i, :])),
                   waits=w, inc=s_xld, inc_n=16)
        p.op("scalar", L(lambda e, i=i: (
            e.activation(out=scr_bf[:], in_=xtv[:, i % 2, :], func=AF.Copy,
                         accum_out=ssum[:, i:i + 1]),
            e.activation(out=scr_bf[:], in_=xtv[:, i % 2, :], func=AF.Square,
                         accum_out=sssq[:, i:i + 1]))[-1]),
            waits=[(s_xld, xv_)], inc=s_stat)
    def post1(e):
        p.ss("vector", e, e.memset(ones64[:], 1.0))
        p.ss("vector", e, e.memset(cneg[:], -24.0))
        p.ss("vector", e, e.memset(cscale[:], 0.125))
        p.ss("vector", e, e.memset(vAv[:, :, :, 64:65], 1.0))
        p.ss("vector", e, e.tensor_scalar(out=smu[:], in0=ssum[:], scalar1=1.0 / D, scalar2=None, op0=ALU.mult))
        p.ss("vector", e, e.tensor_tensor(out=srstd[:], in0=smu[:], in1=smu[:], op=ALU.mult))
        p.ss("vector", e, e.tensor_scalar(out=srstd[:], in0=srstd[:], scalar1=-1e-5, scalar2=None, op0=ALU.add))
        return e.scalar_tensor_tensor(out=srstd[:], in0=sssq[:], scalar=1.0 / D,
                                      in1=srstd[:], op0=ALU.mult, op1=ALU.subtract)
    p.op("vector", post1, waits=[(s_stat, NT), (s_ld, ld_all)], inc=s_post)
    p.op("scalar", lambda e: e.activation(out=srstd[:], in_=srstd[:], func=AF.Sqrt),
         waits=[(s_post, 1)], inc=s_sqrt)
    def post2(e):
        p.ss("vector", e, e.reciprocal(out=srstd[:], in_=srstd[:]))
        p.ss("vector", e, e.tensor_tensor(out=snmu[:], in0=smu[:], in1=srstd[:], op=ALU.mult))
        return e.tensor_scalar(out=snmu[:], in0=snmu[:], scalar1=-1.0, scalar2=None, op0=ALU.mult)
    p.op("vector", post2, waits=[(s_sqrt, 1)], inc=s_post)

    s_xld2 = p.sem("xld2")
    s_apply = p.sem("apply")
    s_tr = p.sem("tr")
    s_trev = p.sem("trev")
    s_pj = p.sem("pj")
    s_pje = p.sem("pje")
    s_rms = p.sem("rms")
    s_rsq = p.sem("rsq")
    s_qtr = p.sem("qtr")
    s_qte = p.sem("qte")
    s_qk = p.sem("qk")
    s_exp = p.sem("exp")
    s_msk = p.sem("msk")
    s_av = p.sem("av")
    s_ye = p.sem("ye")
    s_rc = p.sem("rc")
    s_rpl = p.sem("rpl")
    s_nrm = p.sem("nrm")
    s_pr = p.sem("pr")
    s_pre = p.sem("pre")
    s_xs = p.sem("xs")
    s_sto = p.sem("sto")

    for i in range(NT):
        w = [(s_apply, i - 1)] if i >= 2 else []
        xv_ = p.op("scalar", L(lambda e, i=i: e.dma_start(out=xtv[:, i % 2, :], in_=xsrc[:, i, :])),
                   waits=w, inc=s_xld2, inc_n=16)
        w = [(s_xld2, xv_), (s_post, 2)]
        if i >= 2:
            w.append((s_tr, 2 * (i - 2) + 2))
        p.op("vector", L(lambda e, i=i: e.tensor_scalar(
            out=xnv[:, i % 2, :], in0=xtv[:, i % 2, :],
            scalar1=srstd[:, i:i + 1], scalar2=snmu[:, i:i + 1],
            op0=ALU.mult, op1=ALU.add)), waits=w, inc=s_apply)
        for c4 in (0, 4):
            par = s_tr.v % 2
            w = [(s_apply, s_apply.v)]
            if s_trev.v >= 2:
                w.append((s_trev, s_trev.v - 1))
            def ftr(e, i=i, c4=c4, par=par):
                inst = None
                for j in range(4):
                    inst = e.transpose(ptrp[par][:, j * 128:(j + 1) * 128],
                                       xnv[:, i % 2, (c4 + j) * 128:(c4 + j + 1) * 128],
                                       ident_b[:])
                return inst
            p.op("tensor", ftr, waits=w, inc=s_tr)
            w = [(s_tr, s_tr.v)]
            if i >= 2 and c4 == 0:
                w.append((s_pj, 3 * (i - 1)))
            p.op("scalar", L(lambda e, i=i, c4=c4, par=par: e.activation(
                out=xnTv[:, c4:c4 + 4, (i % 2) * 128:(i % 2 + 1) * 128],
                in_=ptrp[par][:].rearrange("p (c t) -> p c t", t=128),
                func=AF.Copy)), waits=w, inc=s_trev)
        for gi, (col0, width, wsb, m) in enumerate((
                (0, 512, wq_sb, D), (512, 512, wq_sb, D), (0, 512, wkv_sb, 512))):
            par = s_pj.v % 2
            def fp(e, i=i, col0=col0, wsb=wsb, m=m, par=par):
                inst = None
                for c in range(NCH):
                    inst = e.matmul(pq[par][:],
                                    xnTv[:, c, (i % 2) * 128:(i % 2 + 1) * 128],
                                    wsb[:].rearrange("p (c m) -> p c m", m=m)[:, c, col0:col0 + 512],
                                    start=(c == 0), stop=(c == NCH - 1))
                return inst
            w = [(s_trev, s_trev.v)]
            if s_pj.v >= 2:
                w.append((s_pje, s_pje.v - 1))
            p.op("tensor", fp, waits=w, inc=s_pj)
            if gi < 2:
                p.op("scalar", L(lambda e, i=i, gi=gi, par=par: e.activation(
                    out=qkv_[:, i % 2, gi * 512:(gi + 1) * 512], in_=pq[par][:],
                    func=AF.Copy)), waits=[(s_pj, s_pj.v)], inc=s_pje)
            else:
                def fev(e, i=i, par=par):
                    p.ss("scalar", e, e.activation(out=qkv_[:, i % 2, 1024:1280],
                                                   in_=pq[par][:, 0:256], func=AF.Copy))
                    return e.activation(out=vAv[:, i, :, 0:64],
                                        in_=pq[par][:, 256:512].rearrange("p (h f) -> p h f", f=64),
                                        func=AF.Copy)
                p.op("scalar", fev, waits=[(s_pj, s_pj.v)], inc=s_pje)
        p.op("scalar", L(lambda e, i=i: e.activation(
            out=sq_scr[:], in_=qkv_[:, i % 2, :], func=AF.Square)),
            waits=[(s_pje, s_pje.v)], inc=s_rsq)
        def frms2(e, i=i):
            p.ss("vector", e, e.tensor_reduce(
                out=rmsv[:, i % 2, :].rearrange("p (f o) -> p f o", o=1),
                in_=sq_scr[:].rearrange("p (h f) -> p h f", f=64),
                axis=mybir.AxisListType.X, op=ALU.add))
            return e.tensor_scalar(out=rmsv[:, i % 2, :], in0=rmsv[:, i % 2, :],
                                   scalar1=1.0 / 64, scalar2=1.1920929e-07,
                                   op0=ALU.mult, op1=ALU.add)
        p.op("vector", frms2, waits=[(s_rsq, s_rsq.v)], inc=s_rms)
        p.op("scalar", L(lambda e, i=i: e.activation(
            out=rmsv[:, i % 2, :], in_=rmsv[:, i % 2, :], func=AF.Sqrt)),
            waits=[(s_rms, s_rms.v)], inc=s_rsq)
        def frope(e, i=i):
            qv3 = qkv_[:, i % 2, :].rearrange("p (h d) -> p h d", d=64)
            x1 = qv3[:, :, 0:32]
            x2 = qv3[:, :, 32:64]
            cb = ctv[:, i, :].rearrange("p (o x) -> p o x", o=1).broadcast_to([128, 20, 32])
            sb_ = stv[:, i, :].rearrange("p (o x) -> p o x", o=1).broadcast_to([128, 20, 32])
            tA = tmpv[:, 0, :].rearrange("p (h x) -> p h x", x=32)
            tB = tmpv[:, 1, :].rearrange("p (h x) -> p h x", x=32)
            tC = tmpv[:, 2, :].rearrange("p (h x) -> p h x", x=32)
            p.ss("vector", e, e.reciprocal(out=rmsv[:, i % 2, :], in_=rmsv[:, i % 2, :]))
            p.ss("vector", e, e.tensor_tensor(out=tA, in0=x1, in1=cb, op=ALU.mult))
            p.ss("vector", e, e.tensor_tensor(out=tB, in0=x1, in1=sb_, op=ALU.mult))
            p.ss("vector", e, e.tensor_tensor(out=tC, in0=x2, in1=sb_, op=ALU.mult))
            p.ss("vector", e, e.tensor_add(out=x1, in0=tA, in1=tC))
            p.ss("vector", e, e.tensor_tensor(out=tC, in0=x2, in1=cb, op=ALU.mult))
            p.ss("vector", e, e.tensor_sub(out=x2, in0=tC, in1=tB))
            rcb = rmsv[:, i % 2, :].rearrange("p (h o) -> p h o", o=1).broadcast_to([128, 20, 64])
            return e.tensor_tensor(out=qkb_[:, i % 2, :].rearrange("p (h d) -> p h d", d=64),
                                   in0=qv3, in1=rcb, op=ALU.mult)
        p.op("vector", frope, waits=[(s_rsq, s_rsq.v)], inc=s_rms)
        for grp in range(3):
            par = s_qtr.v % 2
            w = [(s_rms, s_rms.v)]
            if s_qte.v >= 2:
                w.append((s_qte, s_qte.v - 1))
            nj = 4 if grp < 2 else 2
            def fqt(e, i=i, grp=grp, par=par, nj=nj):
                inst = None
                for j in range(nj):
                    inst = e.transpose(ptrp[par][:, j * 128:(j + 1) * 128],
                                       qkb_[:, i % 2, (grp * 4 + j) * 128:(grp * 4 + j + 1) * 128],
                                       ident_b[:])
                return inst
            p.op("tensor", fqt, waits=w, inc=s_qtr)
            if grp < 2:
                def fqe(e, i=i, grp=grp, par=par):
                    inst = None
                    for j in range(4):
                        for hh in range(2):
                            h = 2 * (grp * 4 + j) + hh
                            inst = e.activation(
                                out=qTv[_qb_row(h):_qb_row(h) + 64, _qb_chunk(h), i * 128:(i + 1) * 128],
                                in_=ptrp[par][hh * 64:(hh + 1) * 64, j * 128:(j + 1) * 128],
                                func=AF.Copy)
                    return inst
                p.op("scalar", fqe, waits=[(s_qtr, s_qtr.v)], inc=s_qte)
            else:
                p.op("scalar", L(lambda e, i=i, par=par: e.activation(
                    out=kTv[:, 0:2, i * 128:(i + 1) * 128],
                    in_=ptrp[par][:, 0:256].rearrange("p (c t) -> p c t", t=128),
                    func=AF.Copy)), waits=[(s_qtr, s_qtr.v)], inc=s_qte)
    qkv_done = [(s_qte, s_qte.v)]

    for s in range(NT):
        qs = 128 * s
        w = list(qkv_done) if s == 0 else []
        if s >= 2:
            w.append((s_sto, 16 * (s - 1)))
        xv_ = p.op("sync", L(lambda e, s=s: e.dma_start(out=xsv[:, s % 2, :], in_=xsrc[:, s, :])),
                   waits=w, inc=s_xs, inc_n=16)
        if s == 0:
            tiles = [(0, 0)]
        elif s == 1:
            tiles = [(0, None), (1, 0)]
        elif s == 2:
            tiles = [(0, 2), (1, None), (2, 0)]
        else:
            tiles = [("g", None), (s - 2, 1), (s - 1, None), (s, 0)]
        for pr8 in range(8):
            heads = [2 * pr8, 2 * pr8 + 1]
            kv = heads[0] // 4
            rbk = (kv % 2) * 64
            chk = kv // 2
            for kt, mi in tiles:
                glob = (kt == "g")
                for j in (0, 1):
                    h = heads[j]
                    rb = _qb_row(h)
                    ch = _qb_chunk(h)
                    w = list(qkv_done)
                    if s_exp.v > 1:
                        w.append((s_exp, s_exp.v - 1))
                    if glob:
                        p.op("tensor", L(lambda e, rbk=rbk, chk=chk, rb=rb, ch=ch, j=j, qs=qs:
                                         e.matmul(pq[j][0:64, 0:128],
                                                  kTv[rbk:rbk + 64, chk, 0:64],
                                                  qTv[rb:rb + 64, ch, qs:qs + 128],
                                                  start=True, stop=True,
                                                  tile_position=(rbk, 0))),
                             waits=w, inc=s_qk)
                    else:
                        p.op("tensor", L(lambda e, rbk=rbk, chk=chk, rb=rb, ch=ch, j=j, kt=kt, qs=qs:
                                         e.matmul(pq[j][:, 0:128],
                                                  kTv[rbk:rbk + 64, chk, kt * 128:(kt + 1) * 128],
                                                  qTv[rb:rb + 64, ch, qs:qs + 128],
                                                  start=True, stop=True,
                                                  tile_position=(rbk, 0))),
                             waits=w, inc=s_qk)
                for j in (0, 1):
                    w = [(s_qk, s_qk.v - 1 + j)]
                    if s_av.v > 1:
                        w.append((s_av, s_av.v - 1))
                    np_ = 64 if glob else 128
                    p.op("scalar", L(lambda e, j=j, np_=np_: e.activation(
                        out=ptv[0:np_, j, :], in_=pq[j][0:np_, 0:128], func=AF.Exp,
                        bias=cneg[0:np_, :], scale=cscale[0:np_, :])),
                        waits=w, inc=s_exp)
                    if mi is not None:
                        p.op("vector", L(lambda e, j=j, mi=mi: e.tensor_mul(
                            out=ptv[:, j, :], in0=ptv[:, j, :], in1=mskv[:, mi, :])),
                            waits=[(s_exp, s_exp.v)], inc=s_msk)
                    w = [(s_msk, s_msk.v)] if mi is not None else [(s_exp, s_exp.v)]
                    first = (kt == tiles[0][0])
                    last = (kt == tiles[-1][0])
                    if first:
                        w += [(s_ye, s_ye.v), (s_rc, s_rc.v)]
                    if glob:
                        p.op("tensor", L(lambda e, j=j, kv=kv, last=last:
                                         e.matmul(pav[j][:], vAv[0:64, 0, kv, :], ptv[0:64, j, :],
                                                  start=True, stop=last)),
                             waits=w, inc=s_av)
                    else:
                        p.op("tensor", L(lambda e, j=j, kt=kt, kv=kv, first=first, last=last:
                                         e.matmul(pav[j][:], vAv[:, kt, kv, :], ptv[:, j, :],
                                                  start=first, stop=last)),
                             waits=w, inc=s_av)
            for j in (0, 1):
                h = heads[j]
                rb = (h % 2) * 64
                ch = h // 2
                p.op("scalar", L(lambda e, rb=rb, ch=ch, j=j: e.activation(
                    out=yTv[rb:rb + 64, ch, :], in_=pav[j][0:64, :], func=AF.Copy)),
                    waits=[(s_av, s_av.v - 1 + j), (s_pr, s_pr.v)], inc=s_ye)
                p.op("vector", L(lambda e, j=j: e.reciprocal(
                    out=rc_sb[64:65, :], in_=pav[j][64:65, :])),
                    waits=[(s_av, s_av.v - 1 + j), (s_rpl, s_rpl.v)], inc=s_rc)
                par = s_rpl.v % 2
                w = [(s_rc, s_rc.v)]
                if s_rpl.v >= 2:
                    w.append((s_nrm, s_nrm.v - 1))
                p.op("tensor", L(lambda e, par=par: e.matmul(
                    ppr[par][0:64, 0:128], ones64[64:65, :], rc_sb[64:65, :],
                    start=True, stop=True, tile_position=(64, 0))),
                    waits=w, inc=s_rpl)
                p.op("vector", L(lambda e, rb=rb, ch=ch, par=par: e.tensor_mul(
                    out=yTv[rb:rb + 64, ch, :],
                    in0=yTv[rb:rb + 64, ch, :],
                    in1=ppr[par][0:64, 0:128])),
                    waits=[(s_rpl, s_rpl.v), (s_ye, s_ye.v)], inc=s_nrm)
        for mh in range(2):
            par = s_pr.v % 2
            def fpr(e, mh=mh, par=par):
                inst = None
                for c in range(NCH):
                    inst = e.matmul(ppr[par][:], yTv[:, c, :],
                                    wo_sb[:].rearrange("p (c m) -> p c m", m=D)[:, c, mh * 512:(mh + 1) * 512],
                                    start=(c == 0), stop=(c == NCH - 1))
                return inst
            w = [(s_nrm, s_nrm.v)]
            if s_pr.v >= 2:
                w.append((s_pre, s_pre.v - 1))
            p.op("tensor", fpr, waits=w, inc=s_pr)
            p.op("vector", L(lambda e, s=s, mh=mh, par=par: e.scalar_tensor_tensor(
                out=xsv[:, s % 2, mh * 512:(mh + 1) * 512],
                in0=ppr[par][:], scalar=1.0,
                in1=xsv[:, s % 2, mh * 512:(mh + 1) * 512],
                op0=ALU.mult, op1=ALU.add)),
                waits=[(s_pr, s_pr.v), (s_xs, xv_)], inc=s_pre)
        p.op("sync", L(lambda e, s=s: e.dma_start(
            out=xdst[:, s, :], in_=xsv[:, s % 2, :])),
            waits=[(s_pre, s_pre.v)], inc=s_sto, inc_n=16)
    if dbg is not None:
        dscr = sbt(nc, ctx, "dbgb_scr", [128, 96], F32)
        def fdbg(e):
            e.tensor_copy(out=dscr[:, 0:NT], in_=smu[:])
            return e.tensor_copy(out=dscr[:, NT:2 * NT], in_=srstd[:])
        p.op("vector", fdbg, waits=[(s_sto, s_sto.v)], inc=s_pre)
        p.op("sync", L(lambda e: e.dma_start(out=dbg, in_=dscr[:])),
             waits=[(s_pre, s_pre.v)], inc=s_sto, inc_n=16)
    p.op("sync", lambda e: None, waits=[(s_sto, s_sto.v)])
    return (s_sto, s_sto.v)


def build_prog_attn_b():
    nc = bass.Bass("TRN2", target_bir_lowering=False)
    ctx = ExitStack()
    x_in = nc.dram_tensor("x_in", [T, D], F32, kind="ExternalInput").ap()
    Wq = nc.dram_tensor("Wq", [D, D], F32, kind="ExternalInput").ap()
    Wk = nc.dram_tensor("Wk", [D, 256], F32, kind="ExternalInput").ap()
    Wv = nc.dram_tensor("Wv", [D, 256], F32, kind="ExternalInput").ap()
    Wo = nc.dram_tensor("Wo", [D, D], F32, kind="ExternalInput").ap()
    identb = nc.dram_tensor("identb", [128, 128], BF16, kind="ExternalInput").ap()
    bmasks = nc.dram_tensor("bmasks", [3 * 128, 128], BF16, kind="ExternalInput").ap()
    ctok = nc.dram_tensor("ctok", [128, NT * 32], F32, kind="ExternalInput").ap()
    stok = nc.dram_tensor("stok", [128, NT * 32], F32, kind="ExternalInput").ap()
    x_out = nc.dram_tensor("x_out", [T, D], F32, kind="ExternalOutput").ap()

    p = Prog(nc, ctx)
    emit_attn_b(p, nc, ctx, x_in=x_in, Wq=Wq, Wk=Wk, Wv=Wv, Wo=Wo, identb=identb, bmasks=bmasks, ctok=ctok, stok=stok, x_out=x_out)
    p.build()
    ctx.close()
    return nc



def emit_slstm(p, nc, ctx, x_in, lnw, lnb, Wg, bg, Wp, bp, identb, onesb, x_out):
    L = lambda fn: fn
    s_ld = p.gsem("ld")
    EPS = 1e-5

    ident_b = sbt(nc, ctx, "identb_sb", [128, 128], BF16)
    ones_b = sbt(nc, ctx, "ones_sb", [1, 128], BF16)
    gam_sb = sbt(nc, ctx, "gam_sb", [128, NCH], F32)
    bet_sb = sbt(nc, ctx, "bet_sb", [128, NCH], BF16)
    wg_raw = sbt(nc, ctx, "wg_raw", [64, 4 * D], BF16)
    wgT = sbt(nc, ctx, "wgT", [128, 4 * NCH * 64], BF16)
    bias4 = sbt(nc, ctx, "bias4", [64, 4], F32)
    wp_raw = sbt(nc, ctx, "wp_raw", [128, NCH * 64], BF16)
    wpT = sbt(nc, ctx, "wpT", [64, NCH * 128], BF16)
    bp_sb = sbt(nc, ctx, "bp_sb", [1, D], BF16)
    xt_ring = sbt(nc, ctx, "xt_ring", [128, 2 * D], F32)
    xn_ring = sbt(nc, ctx, "xn_ring", [128, 2 * D], BF16)
    xnT_ring = sbt(nc, ctx, "xnT_ring", [128, NCH * 256], BF16)
    gates = {g: sbt(nc, ctx, f"g_{g}", [64, T], F32) for g in "ifzo"}
    m_sb = sbt(nc, ctx, "m_sb", [64, T], F32)
    h_sb = sbt(nc, ctx, "h_sb", [64, T], BF16)
    xs_ring = sbt(nc, ctx, "xs_ring", [128, 2 * D], F32)

    ptrp = [pst(nc, ctx, f"ptr{k}", [128, 512], BF16) for k in range(2)]
    pq = [pst(nc, ctx, f"pq{k}", [64, 512]) for k in range(2)]
    ppr = [pst(nc, ctx, f"ppr{k}", [128, 512]) for k in range(2)]
    pb1 = pst(nc, ctx, "pb1", [64, 4], F32)

    p.op("gpsimd", lambda e: e.dma_start(out=ident_b[:], in_=identb), inc=s_ld, inc_n=16)
    p.op("gpsimd", lambda e: e.dma_start(out=ones_b[:], in_=onesb), inc=s_ld, inc_n=16)
    p.op("gpsimd", lambda e: e.dma_start(
        out=gam_sb[:], in_=lnw.rearrange("o (c p) -> p (o c)", p=128),
        allow_slow_non_contiguous=True), inc=s_ld, inc_n=16)
    p.op("gpsimd", lambda e: e.dma_start(
        out=bet_sb[:], in_=lnb.rearrange("o (c p) -> p (o c)", p=128),
        allow_slow_non_contiguous=True), inc=s_ld, inc_n=16)
    wgr = wg_raw[:].rearrange("p (g m) -> p g m", m=D)
    for gi_, g in enumerate("ifzo"):
        p.op("gpsimd", L(lambda e, gi_=gi_, g=g: e.dma_start(
            out=wgr[:, gi_, :], in_=Wg[g])), inc=s_ld, inc_n=16)
        p.op("gpsimd", L(lambda e, gi_=gi_, g=g: e.dma_start(
            out=bias4[:, gi_:gi_ + 1], in_=bg[g])), inc=s_ld, inc_n=16)
    p.op("gpsimd", lambda e: e.dma_start(
        out=wp_raw[:].rearrange("p (c m) -> p c m", m=64),
        in_=Wp.rearrange("(c p) m -> p c m", p=128)), inc=s_ld, inc_n=16)
    p.op("gpsimd", lambda e: e.dma_start(out=bp_sb[:], in_=bp), inc=s_ld, inc_n=16)
    ld_all = s_ld.v

    wgTv = wgT[:].rearrange("p (g c m) -> p g c m", c=NCH, m=64)
    wpTv = wpT[:].rearrange("p (c m) -> p c m", m=128)
    xsrc = x_in.rearrange("(i p) c -> p i c", p=128)
    xdst = x_out.rearrange("(i p) c -> p i c", p=128)
    xtv = xt_ring[:].rearrange("p (i c) -> p i c", c=D)
    xnv = xn_ring[:].rearrange("p (i c) -> p i c", c=D)
    xnTv = xnT_ring[:].rearrange("p (c t) -> p c t", t=256)
    xsv = xs_ring[:].rearrange("p (i c) -> p i c", c=D)

    s_wt = p.sem("wt")
    s_wte = p.sem("wte")
    for gi_ in range(4):
        for c in range(NCH):
            k = gi_ * NCH + c
            par = k % 2
            w = [(s_ld, ld_all)]
            if s_wt.v >= 2:
                w.append((s_wte, s_wte.v - 1))
            p.op("tensor", L(lambda e, gi_=gi_, c=c, par=par: e.transpose(
                ptrp[par][:, 0:64], wgr[0:64, gi_, c * 128:(c + 1) * 128],
                ident_b[0:64, 0:64])), waits=w, inc=s_wt)
            p.op("vector", L(lambda e, gi_=gi_, c=c, par=par: e.tensor_copy(
                out=wgTv[:, gi_, c, :], in_=ptrp[par][:, 0:64])),
                waits=[(s_wt, s_wt.v)], inc=s_wte)
    for c in range(NCH):
        par = s_wt.v % 2
        w = []
        if s_wt.v >= 2:
            w.append((s_wte, s_wte.v - 1))
        p.op("tensor", L(lambda e, c=c, par=par: e.transpose(
            ptrp[par][0:64, 0:128], wp_raw[:].rearrange("p (c m) -> p c m", m=64)[:, c, :],
            ident_b[:])), waits=w, inc=s_wt)
        p.op("vector", L(lambda e, c=c, par=par: e.tensor_copy(
            out=wpTv[:, c, :], in_=ptrp[par][0:64, 0:128])),
            waits=[(s_wt, s_wt.v)], inc=s_wte)
    s_bf = p.sem("bf")
    def fbias(e):
        inst = None
        for gi_ in range(4):
            for c in range(NCH):
                inst = e.matmul(pb1[:, gi_:gi_ + 1], wgTv[:, gi_, c, :], bet_sb[:, c:c + 1],
                                start=(c == 0), stop=(c == NCH - 1))
        return inst
    p.op("tensor", fbias, waits=[(s_wte, s_wte.v)], inc=s_bf)
    def fbias2(e):
        p.ss("vector", e, e.tensor_add(out=bias4[:], in0=bias4[:], in1=pb1[:]))
        inst = None
        for gi_ in range(4):
            for c in range(NCH):
                inst = e.tensor_scalar(out=wgTv[:, gi_, c, :], in0=wgTv[:, gi_, c, :],
                                       scalar1=gam_sb[:, c:c + 1], scalar2=None, op0=ALU.mult)
        return inst
    p.op("vector", fbias2, waits=[(s_bf, 1)], inc=s_bf)

    ssum = sbt(nc, ctx, "st_sum", [128, NT])
    sssq = sbt(nc, ctx, "st_ssq", [128, NT])
    smu = sbt(nc, ctx, "st_mu", [128, NT])
    srstd = sbt(nc, ctx, "st_rstd", [128, NT])
    snmu = sbt(nc, ctx, "st_nmu", [128, NT])
    sv2 = sbt(nc, ctx, "st_v2", [128, NT])
    scr_bf = sbt(nc, ctx, "st_scr", [128, D], BF16)
    s_xld = p.sem("xld")
    s_stat = p.sem("stat")
    s_post = p.sem("post")
    s_sqrt = p.sem("sqrt")
    for i in range(NT):
        w = [(s_stat, i - 1)] if i >= 2 else []
        xv_ = p.op("sync", L(lambda e, i=i: e.dma_start(out=xtv[:, i % 2, :], in_=xsrc[:, i, :])),
                   waits=w, inc=s_xld, inc_n=16)
        p.op("scalar", L(lambda e, i=i: (
            e.activation(out=scr_bf[:], in_=xtv[:, i % 2, :], func=AF.Copy,
                         accum_out=ssum[:, i:i + 1]),
            e.activation(out=scr_bf[:], in_=xtv[:, i % 2, :], func=AF.Square,
                         accum_out=sssq[:, i:i + 1]))[-1]),
            waits=[(s_xld, xv_)], inc=s_stat)
    def post1(e):
        p.ss("vector", e, e.tensor_scalar(out=smu[:], in0=ssum[:], scalar1=1.0 / D, scalar2=None, op0=ALU.mult))
        p.ss("vector", e, e.tensor_tensor(out=srstd[:], in0=smu[:], in1=smu[:], op=ALU.mult))
        p.ss("vector", e, e.tensor_scalar(out=srstd[:], in0=srstd[:], scalar1=-EPS, scalar2=None, op0=ALU.add))
        return e.scalar_tensor_tensor(out=srstd[:], in0=sssq[:], scalar=1.0 / D,
                                      in1=srstd[:], op0=ALU.mult, op1=ALU.subtract)
    p.op("vector", post1, waits=[(s_stat, NT), (s_ld, ld_all)], inc=s_post)
    p.op("scalar", lambda e: e.activation(out=srstd[:], in_=srstd[:], func=AF.Sqrt),
         waits=[(s_post, 1)], inc=s_sqrt)
    def post2(e):
        p.ss("vector", e, e.reciprocal(out=srstd[:], in_=srstd[:]))
        p.ss("vector", e, e.tensor_tensor(out=sv2[:], in0=srstd[:], in1=srstd[:], op=ALU.mult))
        return e.tensor_scalar(out=sv2[:], in0=sv2[:], scalar1=-EPS, scalar2=1.0 + EPS,
                               op0=ALU.mult, op1=ALU.add)
    p.op("vector", post2, waits=[(s_sqrt, 1)], inc=s_post)
    p.op("scalar", lambda e: e.activation(out=sv2[:], in_=sv2[:], func=AF.Sqrt),
         waits=[(s_post, 2)], inc=s_sqrt)
    def post3(e):
        p.ss("vector", e, e.reciprocal(out=sv2[:], in_=sv2[:]))
        p.ss("vector", e, e.tensor_tensor(out=srstd[:], in0=srstd[:], in1=sv2[:], op=ALU.mult))
        p.ss("vector", e, e.tensor_tensor(out=snmu[:], in0=smu[:], in1=srstd[:], op=ALU.mult))
        return e.tensor_scalar(out=snmu[:], in0=snmu[:], scalar1=-1.0, scalar2=None, op0=ALU.mult)
    p.op("vector", post3, waits=[(s_sqrt, 2)], inc=s_post)

    s_xld2 = p.sem("xld2")
    s_apply = p.sem("apply")
    s_tr = p.sem("tr")
    s_trev = p.sem("trev")
    s_pj = p.sem("pj")
    s_pje = p.sem("pje")
    for i in range(NT):
        w = [(s_apply, i - 1)] if i >= 2 else []
        xv_ = p.op("scalar", L(lambda e, i=i: e.dma_start(out=xtv[:, i % 2, :], in_=xsrc[:, i, :])),
                   waits=w, inc=s_xld2, inc_n=16)
        w = [(s_xld2, xv_), (s_post, 3)]
        if i >= 2:
            w.append((s_tr, 2 * (i - 2) + 2))
        p.op("vector", L(lambda e, i=i: e.tensor_scalar(
            out=xnv[:, i % 2, :], in0=xtv[:, i % 2, :],
            scalar1=srstd[:, i:i + 1], scalar2=snmu[:, i:i + 1],
            op0=ALU.mult, op1=ALU.add)), waits=w, inc=s_apply)
        for c4 in (0, 4):
            par = s_tr.v % 2
            w = [(s_apply, s_apply.v)]
            if s_trev.v >= 2:
                w.append((s_trev, s_trev.v - 1))
            def ftr(e, i=i, c4=c4, par=par):
                inst = None
                for j in range(4):
                    inst = e.transpose(ptrp[par][:, j * 128:(j + 1) * 128],
                                       xnv[:, i % 2, (c4 + j) * 128:(c4 + j + 1) * 128],
                                       ident_b[:])
                return inst
            p.op("tensor", ftr, waits=w, inc=s_tr)
            w = [(s_tr, s_tr.v)]
            if i >= 2 and c4 == 0:
                w.append((s_pj, 4 * ((i - 2) // 2 + 1)))
            p.op("scalar", L(lambda e, i=i, c4=c4, par=par: e.activation(
                out=xnTv[:, c4:c4 + 4, (i % 2) * 128:(i % 2 + 1) * 128],
                in_=ptrp[par][:].rearrange("p (c t) -> p c t", t=128),
                func=AF.Copy)), waits=w, inc=s_trev)
        if i % 2 == 1:
            span = i // 2
            for gi_, g in enumerate("ifzo"):
                par = s_pj.v % 2
                def fp(e, gi_=gi_, par=par):
                    inst = None
                    for c in range(NCH):
                        inst = e.matmul(pq[par][:, 0:256], wgTv[:, gi_, c, :],
                                        xnTv[:, c, :],
                                        start=(c == 0), stop=(c == NCH - 1))
                    return inst
                w = [(s_trev, s_trev.v), (s_bf, 2)]
                if s_pj.v >= 2:
                    w.append((s_pje, s_pje.v - 1))
                p.op("tensor", fp, waits=w, inc=s_pj)
                if g in "if":
                    p.op("scalar", L(lambda e, g=g, gi_=gi_, span=span, par=par: e.activation(
                        out=gates[g][:, span * 256:(span + 1) * 256],
                        in_=pq[par][:, 0:256], func=AF.Identity,
                        bias=bias4[:, gi_:gi_ + 1])), waits=[(s_pj, s_pj.v)], inc=s_pje)
                    p.op("vector", L(lambda e, g=g, span=span: e.tensor_scalar(
                        out=gates[g][:, span * 256:(span + 1) * 256],
                        in0=gates[g][:, span * 256:(span + 1) * 256],
                        scalar1=20.0, scalar2=-20.0, op0=ALU.min, op1=ALU.max)),
                        waits=[(s_pje, s_pje.v)], inc=s_pje)
                else:
                    fn_ = AF.Tanh if g == "z" else AF.Sigmoid
                    p.op("scalar", L(lambda e, g=g, gi_=gi_, span=span, par=par, fn_=fn_:
                                     e.activation(out=gates[g][:, span * 256:(span + 1) * 256],
                                                  in_=pq[par][:, 0:256], func=fn_,
                                                  bias=bias4[:, gi_:gi_ + 1])),
                         waits=[(s_pj, s_pj.v)], inc=s_pje)

    s_sc = p.sem("sc")
    def fscan(e):
        li, lf, z_, o_ = gates["i"], gates["f"], gates["z"], gates["o"]
        p.ss("vector", e, e.tensor_tensor_scan(out=m_sb[0:64, :], data0=lf[0:64, :],
                                               data1=li[0:64, :], initial=0.0,
                                               op0=ALU.add, op1=ALU.max))
        p.ss("vector", e, e.tensor_sub(out=lf[0:64, 0:1], in0=lf[0:64, 0:1], in1=m_sb[0:64, 0:1]))
        p.ss("vector", e, e.tensor_add(out=lf[0:64, 1:T], in0=lf[0:64, 1:T], in1=m_sb[0:64, 0:T - 1]))
        p.ss("vector", e, e.tensor_sub(out=lf[0:64, 1:T], in0=lf[0:64, 1:T], in1=m_sb[0:64, 1:T]))
        return e.tensor_sub(out=li[0:64, :], in0=li[0:64, :], in1=m_sb[0:64, :])
    p.op("vector", fscan, waits=[(s_pje, s_pje.v)], inc=s_sc)
    def fexp(e):
        p.ss("scalar", e, e.activation(out=gates["f"][0:64, :], in_=gates["f"][0:64, :], func=AF.Exp))
        return e.activation(out=gates["i"][0:64, :], in_=gates["i"][0:64, :], func=AF.Exp)
    p.op("scalar", fexp, waits=[(s_sc, 1)], inc=s_sc)
    def fscan2(e):
        li, lf, z_, o_ = gates["i"], gates["f"], gates["z"], gates["o"]
        p.ss("vector", e, e.tensor_tensor(out=z_[0:64, :], in0=z_[0:64, :], in1=li[0:64, :], op=ALU.mult))
        return e.tensor_tensor_scan(out=m_sb[0:64, :], data0=lf[0:64, :],
                                    data1=z_[0:64, :], initial=0.0,
                                    op0=ALU.mult, op1=ALU.add)
    p.op("vector", fscan2, waits=[(s_sc, 2)], inc=s_sc)
    p.op("scalar", lambda e: e.activation(out=m_sb[0:64, :], in_=m_sb[0:64, :], func=AF.Tanh),
         waits=[(s_sc, 3)], inc=s_sc)
    p.op("vector", lambda e: e.tensor_tensor(out=h_sb[0:64, :], in0=gates["o"][0:64, :],
                                             in1=m_sb[0:64, :], op=ALU.mult),
         waits=[(s_sc, 4)], inc=s_sc)

    s_xs = p.sem("xs")
    s_pr = p.sem("pr")
    s_pre = p.sem("pre")
    s_sto = p.sem("sto")
    for i in range(NT):
        w = [(s_sc, 5)] if i == 0 else []
        if i >= 2:
            w.append((s_sto, 16 * (i - 1)))
        xv_ = p.op("sync", L(lambda e, i=i: e.dma_start(out=xsv[:, i % 2, :], in_=xsrc[:, i, :])),
                   waits=w, inc=s_xs, inc_n=16)
        for mh in range(2):
            par = s_pr.v % 2
            def fpr(e, i=i, mh=mh, par=par):
                e.matmul(ppr[par][:], h_sb[0:64, i * 128:(i + 1) * 128],
                         wpT[0:64, mh * 512:(mh + 1) * 512],
                         start=True, stop=False)
                return e.matmul(ppr[par][:], ones_b[0:1, :], bp_sb[0:1, mh * 512:(mh + 1) * 512],
                                start=False, stop=True)
            w = [(s_sc, 5)]
            if s_pr.v >= 2:
                w.append((s_pre, s_pre.v - 1))
            p.op("tensor", fpr, waits=w, inc=s_pr)
            p.op("vector", L(lambda e, i=i, mh=mh, par=par: e.scalar_tensor_tensor(
                out=xsv[:, i % 2, mh * 512:(mh + 1) * 512],
                in0=ppr[par][:], scalar=1.0,
                in1=xsv[:, i % 2, mh * 512:(mh + 1) * 512],
                op0=ALU.mult, op1=ALU.add)),
                waits=[(s_pr, s_pr.v), (s_xs, xv_)], inc=s_pre)
        p.op("sync", L(lambda e, i=i: e.dma_start(out=xdst[:, i, :], in_=xsv[:, i % 2, :])),
             waits=[(s_pre, s_pre.v)], inc=s_sto, inc_n=16)
    p.op("sync", lambda e: None, waits=[(s_sto, s_sto.v)])
    return (s_sto, s_sto.v)


def build_prog_slstm():
    nc = bass.Bass("TRN2", target_bir_lowering=False)
    ctx = ExitStack()
    x_in = nc.dram_tensor("x_in", [T, D], F32, kind="ExternalInput").ap()
    lnw = nc.dram_tensor("lnw", [1, D], F32, kind="ExternalInput").ap()
    lnb = nc.dram_tensor("lnb", [1, D], F32, kind="ExternalInput").ap()
    Wg = {g: nc.dram_tensor(f"W{g}", [64, D], F32, kind="ExternalInput").ap()
          for g in "ifzo"}
    bg = {g: nc.dram_tensor(f"b{g}", [64, 1], F32, kind="ExternalInput").ap()
          for g in "ifzo"}
    Wp = nc.dram_tensor("Wp", [D, 64], F32, kind="ExternalInput").ap()
    bp = nc.dram_tensor("bp", [1, D], F32, kind="ExternalInput").ap()
    identb = nc.dram_tensor("identb", [128, 128], BF16, kind="ExternalInput").ap()
    onesb = nc.dram_tensor("onesb", [1, 128], BF16, kind="ExternalInput").ap()
    x_out = nc.dram_tensor("x_out", [T, D], F32, kind="ExternalOutput").ap()

    p = Prog(nc, ctx)
    emit_slstm(p, nc, ctx, x_in=x_in, lnw=lnw, lnb=lnb, Wg=Wg, bg=bg, Wp=Wp, bp=bp, identb=identb, onesb=onesb, x_out=x_out)
    p.build()
    ctx.close()
    return nc



def emit_l1ffn(p, nc, ctx, x_in, wqr, keys, de_t, ue_t, wg_r, wu_r, wd_r, identb, iota64, x_out):
    L = lambda fn: fn
    s_ld = p.gsem("ld")

    ident_b = sbt(nc, ctx, "identb_sb", [128, 128], BF16)
    io64 = sbt(nc, ctx, "io64", [128, 64], F32)
    wq_raw = sbt(nc, ctx, "wq_raw", [128, 2 * D], BF16)
    wqT = sbt(nc, ctx, "wqT", [128, NCH * 256], BF16)
    keys_sb = sbt(nc, ctx, "keys_sb", [128, 8 * 32], BF16)
    keysv_sb = sbt(nc, ctx, "keysv_sb", [128, 2 * 128], BF16)
    wg_raw = sbt(nc, ctx, "wg_raw", [128, 8 * D], BF16)
    wgT = sbt(nc, ctx, "wgT", [128, NCH * 2048], BF16)
    wuT = sbt(nc, ctx, "wuT", [128, NCH * 2048], BF16)
    wdT = sbt(nc, ctx, "wdT", [128, 16 * D], BF16)
    xt_ring = sbt(nc, ctx, "xt_ring", [128, 2 * D], F32)
    xn_ring = sbt(nc, ctx, "xn_ring", [128, 2 * D], BF16)
    xnT_ring = sbt(nc, ctx, "xnT_ring", [128, NCH * 512], BF16)
    gsil = sbt(nc, ctx, "gsil", [128, 16 * 512], BF16)
    hT_ring = sbt(nc, ctx, "hT_ring", [128, 16 * 512], BF16)
    q_tok = sbt(nc, ctx, "q_tok", [128, 256], BF16)
    qTp = sbt(nc, ctx, "qTp", [128, 2 * 128], BF16)
    sim_sb = sbt(nc, ctx, "sim_sb", [128, 2 * 128], F32)
    mx8 = sbt(nc, ctx, "mx8", [128, 32], F32)
    ix8 = sbt(nc, ctx, "ix8", [128, 16], U32)
    ixf = sbt(nc, ctx, "ixf", [128, 16], F32)
    all64 = sbt(nc, ctx, "all64", [128, 64], F32)
    oh64 = sbt(nc, ctx, "oh64", [128, 8 * 64], F32)
    posf = sbt(nc, ctx, "posf", [128, 8], F32)
    pos8 = sbt(nc, ctx, "pos8", [128, 8], U32)
    eidf = sbt(nc, ctx, "eidf", [128, 8], F32)
    eidu = sbt(nc, ctx, "eidu", [128, 32], U32)
    ew8 = sbt(nc, ctx, "ew8", [128, 8], F32)
    sm8 = sbt(nc, ctx, "sm8", [128, 8], F32)
    w8 = sbt(nc, ctx, "w8", [128, 32], F32)
    deg = sbt(nc, ctx, "deg", [128, 2 * D], BF16)
    ueg = sbt(nc, ctx, "ueg", [128, 2 * D], BF16)
    prod = sbt(nc, ctx, "prod", [128, D], BF16)
    scr_bf = sbt(nc, ctx, "scr_bf", [128, D], BF16)
    xs_ring = sbt(nc, ctx, "xs_ring", [128, 4 * D], F32)

    ptrp = [pst(nc, ctx, f"ptr{k}", [128, 512], BF16) for k in range(2)]
    pq = [pst(nc, ctx, f"pq{k}", [128, 512]) for k in range(2)]
    ppr = [pst(nc, ctx, f"ppr{k}", [128, 512]) for k in range(2)]

    p.op("gpsimd", lambda e: e.dma_start(out=ident_b[:], in_=identb), inc=s_ld, inc_n=16)
    p.op("gpsimd", lambda e: e.dma_start(out=io64[:], in_=iota64), inc=s_ld, inc_n=16)
    p.op("gpsimd", lambda e: e.dma_start(
        out=wq_raw[:].rearrange("p (c m) -> p c m", m=D),
        in_=wqr.rearrange("(c p) m -> p c m", p=128)), inc=s_ld, inc_n=16)
    p.op("gpsimd", lambda e: e.dma_start(
        out=keys_sb[:].rearrange("p (g m) -> p g m", m=32),
        in_=keys.rearrange("(g p) m -> p g m", p=128)), inc=s_ld, inc_n=16)
    p.op("gpsimd", lambda e: e.dma_start(
        out=wg_raw[:].rearrange("p (c m) -> p c m", m=D),
        in_=wg_r.rearrange("(c p) m -> p c m", p=128)[:, 0:8, :]), inc=s_ld, inc_n=16)
    ld_wg = s_ld.v

    xsrc = x_in.rearrange("(i p) c -> p i c", p=128)
    xdst = x_out.rearrange("(i p) c -> p i c", p=128)
    xtv = xt_ring[:].rearrange("p (i c) -> p i c", c=D)
    xnv = xn_ring[:].rearrange("p (i c) -> p i c", c=D)
    xnTv = xnT_ring[:].rearrange("p (c t) -> p c t", t=512)
    wgTv = wgT[:].rearrange("p (c m) -> p c m", m=2048)
    wuTv = wuT[:].rearrange("p (c m) -> p c m", m=2048)
    wdTv = wdT[:].rearrange("p (c m) -> p c m", m=D)
    wqTv = wqT[:].rearrange("p (c m) -> p c m", m=256)
    gsv = gsil[:].rearrange("p (c m) -> p c m", m=512)
    hTv = hT_ring[:].rearrange("p (c m) -> p c m", m=512)
    xsv = xs_ring[:].rearrange("p (i c) -> p i c", c=D)
    degv = deg[:].rearrange("p (s c) -> p s c", c=D)
    uegv = ueg[:].rearrange("p (s c) -> p s c", c=D)

    s_wt = p.sem("wt")
    s_wte = p.sem("wte")

    def emit_wT2(srcv, dstv, n_src_chunk, n_dst_chunk, extra, src_off=0, dst_off=0):
        for ic in range(n_src_chunk):
            for dc in range(n_dst_chunk):
                par = s_wt.v % 2
                w = list(extra)
                if s_wt.v >= 2:
                    w.append((s_wte, s_wte.v - 1))
                p.op("tensor", L(lambda e, ic=ic, dc=dc, par=par, srcv=srcv: e.transpose(
                    ptrp[par][:, 0:128], srcv[:, ic, dc * 128:(dc + 1) * 128], ident_b[:])),
                    waits=w, inc=s_wt)
                p.op("vector", L(lambda e, ic=ic, dc=dc, par=par, dstv=dstv: e.tensor_copy(
                    out=dstv[:, dc + dst_off, (ic + src_off) * 128:(ic + src_off + 1) * 128],
                    in_=ptrp[par][:, 0:128])),
                    waits=[(s_wt, s_wt.v)], inc=s_wte)

    wg_rawv = wg_raw[:].rearrange("p (c m) -> p c m", m=D)
    prev_ld = ld_wg
    for wsrc, dstv in ((wg_r, wgTv), (wu_r, wuTv)):
        for rnd in range(2):
            if not (wsrc is wg_r and rnd == 0):
                p.op("gpsimd", L(lambda e, wsrc=wsrc, rnd=rnd: e.dma_start(
                    out=wg_rawv,
                    in_=wsrc.rearrange("(c p) m -> p c m", p=128)[:, rnd * 8:(rnd + 1) * 8, :])),
                    waits=[(s_wte, s_wte.v)], inc=s_ld, inc_n=16)
                prev_ld = s_ld.v
            else:
                pass
            emit_wT2(wg_rawv, dstv, 8, 8, [(s_ld, prev_ld)], src_off=rnd * 8)
    for rnd in range(2):
        p.op("gpsimd", L(lambda e, rnd=rnd: e.dma_start(
            out=wg_raw[:].rearrange("p (c m) -> p c m", m=D),
            in_=wd_r.rearrange("(c p) m -> p c m", p=128)[:, :, rnd * 1024:(rnd + 1) * 1024])),
            waits=[(s_wte, s_wte.v)], inc=s_ld, inc_n=16)
        emit_wT2(wg_raw[:].rearrange("p (c m) -> p c m", m=D), wdTv, 8, 8,
                 [(s_ld, s_ld.v)], dst_off=rnd * 8)
    emit_wT2(wq_raw[:].rearrange("p (c m) -> p c m", m=D), wqT[:].rearrange("p (c m) -> p c m", m=256), 2, 8, [(s_ld, ld_wg)])
    for h in range(4):
        for pp in range(2):
            par = s_wt.v % 2
            w = [(s_ld, ld_wg)]
            if s_wt.v >= 2:
                w.append((s_wte, s_wte.v - 1))
            p.op("tensor", L(lambda e, h=h, pp=pp, par=par: e.transpose(
                ptrp[par][0:32, 0:128],
                keys_sb[:].rearrange("p (g m) -> p g m", m=32)[:, h * 2 + pp, :],
                ident_b[:])), waits=w, inc=s_wt)
            p.op("vector", L(lambda e, h=h, pp=pp, par=par: e.tensor_copy(
                out=keysv_sb[h * 32:(h + 1) * 32, pp * 128:(pp + 1) * 128],
                in_=ptrp[par][0:32, 0:128])), waits=[(s_wt, s_wt.v)], inc=s_wte)
    prep_done = (s_wte, s_wte.v)

    ssum = sbt(nc, ctx, "st_sum", [128, NT])
    sssq = sbt(nc, ctx, "st_ssq", [128, NT])
    smu = sbt(nc, ctx, "st_mu", [128, NT])
    srstd = sbt(nc, ctx, "st_rstd", [128, NT])
    snmu = sbt(nc, ctx, "st_nmu", [128, NT])
    s_xld = p.sem("xld")
    s_stat = p.sem("stat")
    s_post = p.sem("post")
    s_sqrt = p.sem("sqrt")
    for i in range(NT):
        w = [(s_stat, i - 1)] if i >= 2 else []
        xv_ = p.op("sync", L(lambda e, i=i: e.dma_start(out=xtv[:, i % 2, :], in_=xsrc[:, i, :])),
                   waits=w, inc=s_xld, inc_n=16)
        p.op("scalar", L(lambda e, i=i: (
            e.activation(out=scr_bf[:], in_=xtv[:, i % 2, :], func=AF.Copy,
                         accum_out=ssum[:, i:i + 1]),
            e.activation(out=scr_bf[:], in_=xtv[:, i % 2, :], func=AF.Square,
                         accum_out=sssq[:, i:i + 1]))[-1]),
            waits=[(s_xld, xv_)], inc=s_stat)
    def post1(e):
        p.ss("vector", e, e.tensor_scalar(out=smu[:], in0=ssum[:], scalar1=1.0 / D, scalar2=None, op0=ALU.mult))
        p.ss("vector", e, e.tensor_tensor(out=srstd[:], in0=smu[:], in1=smu[:], op=ALU.mult))
        p.ss("vector", e, e.tensor_scalar(out=srstd[:], in0=srstd[:], scalar1=-1e-5, scalar2=None, op0=ALU.add))
        return e.scalar_tensor_tensor(out=srstd[:], in0=sssq[:], scalar=1.0 / D,
                                      in1=srstd[:], op0=ALU.mult, op1=ALU.subtract)
    p.op("vector", post1, waits=[(s_stat, NT)], inc=s_post)
    p.op("scalar", lambda e: e.activation(out=srstd[:], in_=srstd[:], func=AF.Sqrt),
         waits=[(s_post, 1)], inc=s_sqrt)
    def post2(e):
        p.ss("vector", e, e.reciprocal(out=srstd[:], in_=srstd[:]))
        p.ss("vector", e, e.tensor_tensor(out=snmu[:], in0=smu[:], in1=srstd[:], op=ALU.mult))
        return e.tensor_scalar(out=snmu[:], in0=snmu[:], scalar1=-1.0, scalar2=None, op0=ALU.mult)
    p.op("vector", post2, waits=[(s_sqrt, 1)], inc=s_post)

    s_xld2 = p.sem("xld2")
    s_apply = p.sem("apply")
    s_tr = p.sem("tr")
    s_trev = p.sem("trev")
    s_gu = p.sem("gu")
    s_gue = p.sem("gue")
    s_dn = p.sem("dn")
    s_qp = p.sem("qp")
    s_qpe = p.sem("qpe")
    s_tk = p.sem("tk")
    s_tke = p.sem("tke")
    s_g = p.gsem("g")
    s_ew = p.sem("ew")
    s_ewa = p.sem("ewa")
    s_es = p.sem("es")
    s_xs = p.sem("xs")
    s_pre = p.sem("pre")
    s_sto = p.sem("sto")

    for i in range(NT):
        w = [(s_apply, i - 1)] if i >= 2 else []
        xv_ = p.op("scalar", L(lambda e, i=i: e.dma_start(out=xtv[:, i % 2, :], in_=xsrc[:, i, :])),
                   waits=w, inc=s_xld2, inc_n=16)
        w = [(s_xld2, xv_), (s_post, 2)]
        if i >= 2:
            w.append((s_tr, s_tr.v_tile.get(i - 2, 0)) if hasattr(s_tr, 'v_tile') else (s_tr, 0))
        if not hasattr(s_tr, 'v_tile'):
            s_tr.v_tile = {}
        if i >= 2:
            w[-1] = (s_tr, s_tr.v_tile[i - 2])
        wcons = []
        if i >= 2:
            wcons = [(s_ew, 32 * (i - 1))]
        p.op("vector", L(lambda e, i=i: e.tensor_scalar(
            out=xnv[:, i % 2, :], in0=xtv[:, i % 2, :],
            scalar1=srstd[:, i:i + 1], scalar2=snmu[:, i:i + 1],
            op0=ALU.mult, op1=ALU.add)), waits=w + wcons, inc=s_apply)
        for c4 in (0, 4):
            par = s_tr.v % 2
            w = [(s_apply, s_apply.v)] + ([prep_done] if i == 0 and c4 == 0 else [])
            if s_trev.v >= 2:
                w.append((s_trev, s_trev.v - 1))
            def ftr(e, i=i, c4=c4, par=par):
                inst = None
                for j in range(4):
                    inst = e.transpose(ptrp[par][:, j * 128:(j + 1) * 128],
                                       xnv[:, i % 2, (c4 + j) * 128:(c4 + j + 1) * 128],
                                       ident_b[:])
                return inst
            p.op("tensor", ftr, waits=w, inc=s_tr)
            w = [(s_tr, s_tr.v)]
            if i >= 4 and c4 == 0:
                w.append((s_gu, 32 * ((i - 4) // 4 + 1)))
                w.append((s_qp, 9 * (i - 3)))
            p.op("scalar", L(lambda e, i=i, c4=c4, par=par: e.activation(
                out=xnTv[:, c4:c4 + 4, (i % 4) * 128:(i % 4 + 1) * 128],
                in_=ptrp[par][:].rearrange("p (c t) -> p c t", t=128),
                func=AF.Copy)), waits=w, inc=s_trev)
        par = s_qp.v % 2
        def fq(e, i=i, par=par):
            inst = None
            for c in range(NCH):
                inst = e.matmul(pq[par][:, 0:256], xnTv[:, c, (i % 4) * 128:(i % 4 + 1) * 128],
                                wqTv[:, c, :], start=(c == 0), stop=(c == NCH - 1))
            return inst
        w = [(s_trev, s_trev.v)]
        if s_qp.v >= 2:
            w.append((s_qpe, s_qpe.v - 1))
        p.op("tensor", fq, waits=w, inc=s_qp)
        p.op("scalar", L(lambda e, par=par: e.activation(
            out=q_tok[:], in_=pq[par][:, 0:256], func=AF.Copy)),
            waits=[(s_qp, s_qp.v)], inc=s_qpe)
        for half in range(2):
            par = s_tr.v % 2
            w = [(s_qpe, s_qpe.v)]
            if s_trev.v >= 2:
                w.append((s_trev, s_trev.v - 1))
            p.op("tensor", L(lambda e, half=half, par=par: e.transpose(
                ptrp[par][:, 0:128], q_tok[:, half * 128:(half + 1) * 128], ident_b[:])),
                waits=w, inc=s_tr)
            p.op("scalar", L(lambda e, half=half, par=par: e.activation(
                out=qTp[:, half * 128:(half + 1) * 128], in_=ptrp[par][:, 0:128],
                func=AF.Copy)), waits=[(s_tr, s_tr.v)], inc=s_trev)
        s_tr.v_tile[i] = s_tr.v
        for h in range(4):
            for pp in range(2):
                par = s_qp.v % 2
                def fs(e, h=h, pp=pp, par=par):
                    return e.matmul(pq[par][:, 0:128],
                                    qTp[h * 32:(h + 1) * 32, pp * 128:(pp + 1) * 128],
                                    keysv_sb[h * 32:(h + 1) * 32, pp * 128:(pp + 1) * 128],
                                    start=True, stop=True, tile_position=(h * 32, 0))
                w = [(s_trev, s_trev.v)]
                if s_qp.v >= 2:
                    w.append((s_qpe, s_qpe.v - 1))
                p.op("tensor", fs, waits=w, inc=s_qp)
                p.op("vector", L(lambda e, pp=pp, par=par: e.tensor_copy(
                    out=sim_sb[:, pp * 128:(pp + 1) * 128], in_=pq[par][:, 0:128])),
                    waits=[(s_qp, s_qp.v)], inc=s_qpe)
            def ftk(e, i=i, h=h):
                p.ss("vector", e, e.max(out=mx8[:, 0:8], in_=sim_sb[:, 0:128]))
                p.ss("vector", e, e.max_index(out=ix8[:, 0:8], in_max=mx8[:, 0:8],
                                              in_values=sim_sb[:, 0:128]))
                p.ss("vector", e, e.max(out=mx8[:, 8:16], in_=sim_sb[:, 128:256]))
                p.ss("vector", e, e.max_index(out=ix8[:, 8:16], in_max=mx8[:, 8:16],
                                              in_values=sim_sb[:, 128:256]))
                p.ss("vector", e, e.tensor_copy(out=ixf[:], in_=ix8[:]))
                p.ss("vector", e, e.tensor_tensor(
                    out=all64[:].rearrange("p (a b) -> p a b", b=8),
                    in0=mx8[:, 0:8].rearrange("p (a o) -> p a o", o=1).broadcast_to([128, 8, 8]),
                    in1=mx8[:, 8:16].rearrange("p (o b) -> p o b", o=1).broadcast_to([128, 8, 8]),
                    op=ALU.add))
                p.ss("vector", e, e.max(out=mx8[:, 16:24], in_=all64[:]))
                p.ss("vector", e, e.max_index(out=pos8[:], in_max=mx8[:, 16:24],
                                              in_values=all64[:]))
                p.ss("vector", e, e.tensor_copy(out=posf[:], in_=pos8[:]))
                p.ss("vector", e, e.scalar_tensor_tensor(
                    out=all64[:].rearrange("p (a b) -> p a b", b=8),
                    in0=ixf[:, 0:8].rearrange("p (a o) -> p a o", o=1).broadcast_to([128, 8, 8]),
                    scalar=128.0,
                    in1=ixf[:, 8:16].rearrange("p (o b) -> p o b", o=1).broadcast_to([128, 8, 8]),
                    op0=ALU.mult, op1=ALU.add))
                p.ss("vector", e, e.tensor_tensor(
                    out=oh64[:].rearrange("p (r c) -> p r c", c=64),
                    in0=posf[:].rearrange("p (r o) -> p r o", o=1).broadcast_to([128, 8, 64]),
                    in1=io64[:].rearrange("p (o c) -> p o c", o=1).broadcast_to([128, 8, 64]),
                    op=ALU.is_equal))
                p.ss("vector", e, e.tensor_tensor(
                    out=oh64[:].rearrange("p (r c) -> p r c", c=64),
                    in0=oh64[:].rearrange("p (r c) -> p r c", c=64),
                    in1=all64[:].rearrange("p (o c) -> p o c", o=1).broadcast_to([128, 8, 64]),
                    op=ALU.mult))
                p.ss("vector", e, e.tensor_reduce(
                    out=eidf[:].rearrange("p (r o) -> p r o", o=1),
                    in_=oh64[:].rearrange("p (r c) -> p r c", c=64),
                    axis=mybir.AxisListType.X, op=ALU.add))
                p.ss("vector", e, e.tensor_copy(out=eidu[:, h * 8:(h + 1) * 8], in_=eidf[:]))
                return e.tensor_scalar(out=sm8[:], in0=mx8[:, 16:24],
                                       scalar1=mx8[:, 16:17], scalar2=None,
                                       op0=ALU.subtract)
            wtk = [(s_qpe, s_qpe.v)]
            if i >= 1:
                wtk.append((s_g, 32 * 32 * i))
            p.op("vector", ftk, waits=wtk, inc=s_tk)
            p.op("scalar", L(lambda e: e.activation(out=sm8[:], in_=sm8[:], func=AF.Exp)),
                 waits=[(s_tk, s_tk.v)], inc=s_tke)
            def ftk2(e, i=i, h=h):
                p.ss("vector", e, e.tensor_reduce(
                    out=mx8[:, 24:25].rearrange("p (a o) -> p a o", o=1),
                    in_=sm8[:].rearrange("p (o c) -> p o c", o=1),
                    axis=mybir.AxisListType.X, op=ALU.add))
                p.ss("vector", e, e.reciprocal(out=mx8[:, 24:25], in_=mx8[:, 24:25]))
                return e.tensor_scalar(out=w8[:, h * 8:(h + 1) * 8], in0=sm8[:],
                                       scalar1=mx8[:, 24:25], scalar2=None, op0=ALU.mult)
            p.op("vector", ftk2, waits=[(s_tke, s_tke.v)], inc=s_tk)
        w = []
        if i >= 4:
            w.append((s_sto, 16 * (i - 3)))
        xr_ = p.op("sync", L(lambda e, i=i: e.dma_start(out=xsv[:, i % 4, :], in_=xsrc[:, i, :])),
                   waits=w, inc=s_xs, inc_n=16)
        for h in range(4):
            for j in range(8):
                gsl = (h * 8 + j) % 2
                w = [(s_tk, s_tk.v)]
                if s_g.v >= 2 * 16 * 2:
                    w.append((s_ew, s_ew.v - 1))
                    w.append((s_es, s_es.v - 1))
                p.op("gpsimd", L(lambda e, h=h, j=j, gsl=gsl: e.indirect_dma_start(
                    out=degv[:, gsl, :], out_offset=None, in_=de_t,
                    in_offset=bass.IndirectOffsetOnAxis(eidu[:, h * 8 + j:h * 8 + j + 1], 0))),
                    waits=w, inc=s_g, inc_n=16)
                p.op("gpsimd", L(lambda e, h=h, j=j, gsl=gsl: e.indirect_dma_start(
                    out=uegv[:, gsl, :], out_offset=None, in_=ue_t,
                    in_offset=bass.IndirectOffsetOnAxis(eidu[:, h * 8 + j:h * 8 + j + 1], 0))),
                    inc=s_g, inc_n=16)
                p.op("vector", L(lambda e, i=i, gsl=gsl: e.tensor_tensor(
                    out=prod[:], in0=degv[:, gsl, :], in1=xnv[:, i % 2, :], op=ALU.mult)),
                    waits=[(s_g, s_g.v)], inc=s_ew)
                def few(e):
                    p.ss("scalar", e, e.activation(out=scr_bf[:], in_=prod[:], func=AF.Copy,
                                                   accum_out=ew8[:, 0:1]))
                    return e.activation(out=ew8[:, 1:2], in_=ew8[:, 0:1], func=AF.Silu)
                p.op("scalar", few, waits=[(s_ew, s_ew.v)], inc=s_ewa)
                def fes(e, i=i, h=h, j=j, gsl=gsl):
                    p.ss("vector", e, e.tensor_tensor(out=ew8[:, 2:3], in0=ew8[:, 1:2],
                                                      in1=w8[:, h * 8 + j:h * 8 + j + 1],
                                                      op=ALU.mult))
                    return e.scalar_tensor_tensor(
                        out=xsv[:, i % 4, :], in0=uegv[:, gsl, :], scalar=ew8[:, 2:3],
                        in1=xsv[:, i % 4, :], op0=ALU.mult, op1=ALU.add)
                p.op("vector", fes, waits=[(s_ewa, s_ewa.v), (s_xs, xr_)], inc=s_es)
        if i % 4 == 3:
            span = i // 4
            for which, wv_ in (("g", wgTv), ("u", wuTv)):
                for ic in range(16):
                    par = s_gu.v % 2
                    def fgu(e, ic=ic, wv_=wv_, par=par):
                        inst = None
                        for c in range(NCH):
                            inst = e.matmul(pq[par][:],
                                            wv_[:, c, ic * 128:(ic + 1) * 128],
                                            xnTv[:, c, :],
                                            start=(c == 0), stop=(c == NCH - 1))
                        return inst
                    w = [(s_trev, s_trev.v), (s_tk, s_tk.v)]
                    if s_gu.v >= 2:
                        w.append((s_gue, s_gue.v - 1))
                    if span >= 1:
                        w.append((s_dn, 8 * span))
                    p.op("tensor", fgu, waits=w, inc=s_gu)
                    if which == "g":
                        p.op("scalar", L(lambda e, ic=ic, par=par: e.activation(
                            out=gsv[:, ic, :], in_=pq[par][:], func=AF.Silu)),
                            waits=[(s_gu, s_gu.v)], inc=s_gue)
                    else:
                        p.op("vector", L(lambda e, ic=ic, par=par: e.tensor_tensor(
                            out=hTv[:, ic, :],
                            in0=pq[par][:], in1=gsv[:, ic, :],
                            op=ALU.mult)), waits=[(s_gu, s_gu.v)], inc=s_gue)
            for ti4 in range(4):
                gt = 4 * span + ti4
                for mh in range(2):
                    par = s_dn.v % 2
                    def fdn(e, span=span, ti4=ti4, mh=mh, par=par):
                        inst = None
                        for ic in range(16):
                            inst = e.matmul(ppr[par][:],
                                            hTv[:, ic, ti4 * 128:(ti4 + 1) * 128],
                                            wdTv[:, ic, mh * 512:(mh + 1) * 512],
                                            start=(ic == 0), stop=(ic == 15))
                        return inst
                    w = [(s_gue, s_gue.v)]
                    if s_dn.v >= 2:
                        w.append((s_pre, s_pre.v - 1))
                    p.op("tensor", fdn, waits=w, inc=s_dn)
                    p.op("vector", L(lambda e, gt=gt, mh=mh, par=par: e.scalar_tensor_tensor(
                        out=xsv[:, gt % 4, mh * 512:(mh + 1) * 512],
                        in0=ppr[par][:], scalar=1.0,
                        in1=xsv[:, gt % 4, mh * 512:(mh + 1) * 512],
                        op0=ALU.mult, op1=ALU.add)),
                        waits=[(s_dn, s_dn.v), (s_es, 32 * (gt + 1))], inc=s_pre)
                p.op("sync", L(lambda e, gt=gt: e.dma_start(
                    out=xdst[:, gt, :], in_=xsv[:, gt % 4, :])),
                    waits=[(s_pre, s_pre.v)], inc=s_sto, inc_n=16)
    p.op("sync", lambda e: None, waits=[(s_sto, s_sto.v)])
    return (s_sto, s_sto.v)


def build_prog_l1ffn():
    nc = bass.Bass("TRN2", target_bir_lowering=False)
    ctx = ExitStack()
    x_in = nc.dram_tensor("x_in", [T, D], F32, kind="ExternalInput").ap()
    wqr = nc.dram_tensor("wqr", [256, D], F32, kind="ExternalInput").ap()
    keys = nc.dram_tensor("keys", [4 * 128 * 2, 32], F32, kind="ExternalInput").ap()
    de_t = nc.dram_tensor("de_t", [16384, D], F32, kind="ExternalInput").ap()
    ue_t = nc.dram_tensor("ue_t", [16384, D], F32, kind="ExternalInput").ap()
    wg_r = nc.dram_tensor("wg_r", [2048, D], F32, kind="ExternalInput").ap()
    wu_r = nc.dram_tensor("wu_r", [2048, D], F32, kind="ExternalInput").ap()
    wd_r = nc.dram_tensor("wd_r", [D, 2048], F32, kind="ExternalInput").ap()
    identb = nc.dram_tensor("identb", [128, 128], BF16, kind="ExternalInput").ap()
    iota64 = nc.dram_tensor("iota64", [128, 64], F32, kind="ExternalInput").ap()
    x_out = nc.dram_tensor("x_out", [T, D], F32, kind="ExternalOutput").ap()

    p = Prog(nc, ctx)
    emit_l1ffn(p, nc, ctx, x_in=x_in, wqr=wqr, keys=keys, de_t=de_t, ue_t=ue_t, wg_r=wg_r, wu_r=wu_r, wd_r=wd_r, identb=identb, iota64=iota64, x_out=x_out)
    p.build()
    ctx.close()
    return nc



def emit_ssm(p, nc, ctx, x_in, gam, bet, Wd, bd, Wbc, bbc, Alog, Dsk, identb, identf, onesb, x_out, bc_scr, stage=2):
    L = lambda fn: fn
    s_ld = p.gsem("ld")
    EPS = 1e-5
    CH = 512
    NCHK = T // CH
    NST = 16

    ident_b = sbt(nc, ctx, "identb_sb", [128, 128], BF16)
    ident_f = sbt(nc, ctx, "identf_sb", [128, 128], F32)
    ones_b = sbt(nc, ctx, "ones_sb", [1, 128], BF16)
    ones_f = sbt(nc, ctx, "ones_f", [1, 128], F32)
    wd_sb = sbt(nc, ctx, "wd_sb", [128, NCH * D], BF16)
    wbc_sb = sbt(nc, ctx, "wbc_sb", [128, NCH * 32], BF16)
    bd_col = sbt(nc, ctx, "bd_col", [128, NCH], F32)
    escr2 = sbt(nc, ctx, "escr2", [128, 512], F32)
    bbc_sb = sbt(nc, ctx, "bbc_sb", [32, 1], F32)
    A_sb = sbt(nc, ctx, "A_sb", [128, NCH * NST], F32)
    gam_rep = sbt(nc, ctx, "gam_rep", [128, D], F32)
    bet_rep = sbt(nc, ctx, "bet_rep", [128, D], F32)
    dsk_rep = sbt(nc, ctx, "dsk_rep", [128, D], F32)
    xt_ring = sbt(nc, ctx, "xt_ring", [128, 2 * D], F32)
    xnf_ch = sbt(nc, ctx, "xnf_ch", [128, 4 * D], F32)
    xn_bf = sbt(nc, ctx, "xn_bf", [128, 2 * D], BF16)
    xnT_ring = sbt(nc, ctx, "xnT_ring", [128, NCH * 512], BF16)
    xnTf = sbt(nc, ctx, "xnTf", [128, NCH * 512], F32)
    dT_sb = sbt(nc, ctx, "dT_sb", [128, NCH * 512], F32)
    dxT = sbt(nc, ctx, "dxT", [128, NCH * 512], F32)
    g_sb = sbt(nc, ctx, "g_sb", [128, NCH * 512], F32)
    u_sb = sbt(nc, ctx, "u_sb", [128, NCH * 512], F32)
    h_sb = sbt(nc, ctx, "h_sb", [128, NCH * 512], F32)
    y_sb = sbt(nc, ctx, "y_sb", [128, NCH * 512], F32)
    hcar = sbt(nc, ctx, "hcar", [128, NCH * NST], F32)
    brow = sbt(nc, ctx, "brow", [1, 2 * 512], F32)
    crow = sbt(nc, ctx, "crow", [1, 2 * 512], F32)
    escr = sbt(nc, ctx, "escr", [128, 512], F32)
    xs_ring = sbt(nc, ctx, "xs_ring", [128, 2 * D], F32)

    ptrb = [pst(nc, ctx, f"ptrb{k}", [128, 512], BF16) for k in range(2)]
    ptrf = [pst(nc, ctx, f"ptrf{k}", [128, 512], F32) for k in range(2)]
    pq = [pst(nc, ctx, f"pq{k}", [128, 512]) for k in range(2)]
    prpl = [pst(nc, ctx, f"prpl{k}", [128, 512]) for k in range(2)]

    p.op("gpsimd", lambda e: e.dma_start(out=ident_b[:], in_=identb), inc=s_ld, inc_n=16)
    p.op("gpsimd", lambda e: e.dma_start(out=ident_f[:], in_=identf), inc=s_ld, inc_n=16)
    p.op("gpsimd", lambda e: e.dma_start(out=ones_b[:], in_=onesb), inc=s_ld, inc_n=16)
    p.op("gpsimd", lambda e: e.dma_start(out=gam_rep[:], in_=gam.broadcast_to([128, D])), inc=s_ld, inc_n=16)
    p.op("gpsimd", lambda e: e.dma_start(out=bet_rep[:], in_=bet.broadcast_to([128, D])), inc=s_ld, inc_n=16)
    p.op("gpsimd", lambda e: e.dma_start(out=dsk_rep[:], in_=Dsk.broadcast_to([128, D])), inc=s_ld, inc_n=16)
    p.op("gpsimd", lambda e: e.dma_start(
        out=wd_sb[:].rearrange("p (c m) -> p c m", m=D),
        in_=Wd.rearrange("(c p) m -> p c m", p=128)), inc=s_ld, inc_n=16)
    p.op("gpsimd", lambda e: e.dma_start(
        out=wbc_sb[:].rearrange("p (c m) -> p c m", m=32),
        in_=Wbc.rearrange("(c p) m -> p c m", p=128)), inc=s_ld, inc_n=16)
    p.op("gpsimd", lambda e: e.dma_start(
        out=bd_col[:], in_=bd.rearrange("o (c p) -> p (o c)", p=128),
        allow_slow_non_contiguous=True), inc=s_ld, inc_n=16)
    p.op("gpsimd", lambda e: e.dma_start(out=bbc_sb[:], in_=bbc), inc=s_ld, inc_n=16)
    p.op("gpsimd", lambda e: e.dma_start(
        out=A_sb[:].rearrange("p (c m) -> p c m", m=NST),
        in_=Alog.rearrange("(c p) m -> p c m", p=128)), inc=s_ld, inc_n=16)
    ld_all = s_ld.v
    s_prep = p.sem("prep")
    def fprep(e):
        p.ss("vector", e, e.memset(ones_f[:], 1.0))
        p.ss("vector", e, e.memset(hcar[:], 0.0))
        return e.memset(y_sb[:], 0.0)
    p.op("vector", fprep, waits=[(s_ld, ld_all)], inc=s_prep)
    p.op("scalar", lambda e: e.activation(out=A_sb[:], in_=A_sb[:], func=AF.Exp),
         waits=[(s_prep, 1)], inc=s_prep)
    p.op("vector", lambda e: e.tensor_scalar(out=A_sb[:], in0=A_sb[:], scalar1=-1.0,
                                             scalar2=None, op0=ALU.mult),
         waits=[(s_prep, 2)], inc=s_prep)

    xsrc = x_in.rearrange("(i p) c -> p i c", p=128)
    xdst = x_out.rearrange("(i p) c -> p i c", p=128)
    xosrc = x_out.rearrange("(i p) c -> p i c", p=128)
    xtv = xt_ring[:].rearrange("p (i c) -> p i c", c=D)
    xnfv = xnf_ch[:].rearrange("p (i c) -> p i c", c=D)
    xnbv = xn_bf[:].rearrange("p (i c) -> p i c", c=D)
    xnTv = xnT_ring[:].rearrange("p (c t) -> p c t", t=512)
    xnTfv = xnTf[:].rearrange("p (c t) -> p c t", t=512)
    dTv = dT_sb[:].rearrange("p (c t) -> p c t", t=512)
    dxTv = dxT[:].rearrange("p (c t) -> p c t", t=512)
    gv = g_sb[:].rearrange("p (c t) -> p c t", t=512)
    uv = u_sb[:].rearrange("p (c t) -> p c t", t=512)
    hv = h_sb[:].rearrange("p (c t) -> p c t", t=512)
    yv = y_sb[:].rearrange("p (c t) -> p c t", t=512)
    hcv = hcar[:].rearrange("p (c n) -> p c n", n=NST)
    Av = A_sb[:].rearrange("p (c n) -> p c n", n=NST)
    wdv = wd_sb[:].rearrange("p (c m) -> p c m", m=D)
    wbcv = wbc_sb[:].rearrange("p (c m) -> p c m", m=32)
    xsv = xs_ring[:].rearrange("p (i c) -> p i c", c=D)

    ssum = sbt(nc, ctx, "st_sum", [128, NT])
    sssq = sbt(nc, ctx, "st_ssq", [128, NT])
    smu = sbt(nc, ctx, "st_mu", [128, NT])
    srstd = sbt(nc, ctx, "st_rstd", [128, NT])
    srstd_o = sbt(nc, ctx, "st_rstdo", [128, NT])
    snmu = sbt(nc, ctx, "st_nmu", [128, NT])
    snmu_o = sbt(nc, ctx, "st_nmuo", [128, NT])
    sv2 = sbt(nc, ctx, "st_v2", [128, NT])
    scr_bf = sbt(nc, ctx, "st_scr", [128, D], BF16)
    s_xld = p.sem("xld")
    s_stat = p.sem("stat")
    s_post = p.sem("post")
    s_sqrt = p.sem("sqrt")
    for i in range(NT):
        w = [(s_stat, i - 1)] if i >= 2 else []
        xv_ = p.op("sync", L(lambda e, i=i: e.dma_start(out=xtv[:, i % 2, :], in_=xsrc[:, i, :])),
                   waits=w, inc=s_xld, inc_n=16)
        p.op("scalar", L(lambda e, i=i: (
            e.activation(out=scr_bf[:], in_=xtv[:, i % 2, :], func=AF.Copy,
                         accum_out=ssum[:, i:i + 1]),
            e.activation(out=scr_bf[:], in_=xtv[:, i % 2, :], func=AF.Square,
                         accum_out=sssq[:, i:i + 1]))[-1]),
            waits=[(s_xld, xv_)], inc=s_stat)
    def post1(e):
        p.ss("vector", e, e.tensor_scalar(out=smu[:], in0=ssum[:], scalar1=1.0 / D, scalar2=None, op0=ALU.mult))
        p.ss("vector", e, e.tensor_tensor(out=srstd[:], in0=smu[:], in1=smu[:], op=ALU.mult))
        p.ss("vector", e, e.tensor_scalar(out=srstd[:], in0=srstd[:], scalar1=-EPS, scalar2=None, op0=ALU.add))
        return e.scalar_tensor_tensor(out=srstd[:], in0=sssq[:], scalar=1.0 / D,
                                      in1=srstd[:], op0=ALU.mult, op1=ALU.subtract)
    p.op("vector", post1, waits=[(s_stat, NT)], inc=s_post)
    p.op("scalar", lambda e: e.activation(out=srstd[:], in_=srstd[:], func=AF.Sqrt),
         waits=[(s_post, 1)], inc=s_sqrt)
    def post2(e):
        p.ss("vector", e, e.reciprocal(out=srstd[:], in_=srstd[:]))
        p.ss("vector", e, e.tensor_copy(out=srstd_o[:], in_=srstd[:]))
        p.ss("vector", e, e.tensor_tensor(out=sv2[:], in0=srstd[:], in1=srstd[:], op=ALU.mult))
        return e.tensor_scalar(out=sv2[:], in0=sv2[:], scalar1=-EPS, scalar2=1.0 + EPS,
                               op0=ALU.mult, op1=ALU.add)
    p.op("vector", post2, waits=[(s_sqrt, 1)], inc=s_post)
    p.op("scalar", lambda e: e.activation(out=sv2[:], in_=sv2[:], func=AF.Sqrt),
         waits=[(s_post, 2)], inc=s_sqrt)
    def post3(e):
        p.ss("vector", e, e.reciprocal(out=sv2[:], in_=sv2[:]))
        p.ss("vector", e, e.tensor_tensor(out=srstd[:], in0=srstd[:], in1=sv2[:], op=ALU.mult))
        p.ss("vector", e, e.tensor_tensor(out=snmu[:], in0=smu[:], in1=srstd[:], op=ALU.mult))
        p.ss("vector", e, e.tensor_scalar(out=snmu[:], in0=snmu[:], scalar1=-1.0, scalar2=None, op0=ALU.mult))
        p.ss("vector", e, e.tensor_tensor(out=snmu_o[:], in0=smu[:], in1=srstd_o[:], op=ALU.mult))
        return e.tensor_scalar(out=snmu_o[:], in0=snmu_o[:], scalar1=-1.0, scalar2=None, op0=ALU.mult)
    p.op("vector", post3, waits=[(s_sqrt, 2)], inc=s_post)

    s_xld2 = p.sem("xld2")
    s_apply = p.sem("apply")
    s_tr = p.sem("tr")
    s_trevb = p.sem("trevb")
    s_trevf = p.sem("trevf")
    tb_cnt = [0]; tf_cnt = [0]; tb_ev = [0]; tf_ev = [0]
    s_dj = p.sem("dj")
    s_dje = p.sem("dje")
    s_bcd = p.sem("bcd")
    s_gx = p.sem("gx")
    s_sc = p.sem("sc")
    s_yt = p.sem("yt")
    s_xs = p.sem("xs")
    s_pre = p.sem("pre")
    s_sto = p.sem("sto")
    s_sto2 = p.sem("sto2")

    for i in range(NT):
        ch = i // 4
        if stage == 0:
            w = [(s_sto, 16 * (i - 1))] if i >= 2 else []
        else:
            w = [(s_apply, 4 * (i - 2) + 4), (s_sto, 16 * (i - 1))] if i >= 2 else []
        xv_ = p.op("scalar", L(lambda e, i=i: e.dma_start(out=xtv[:, i % 2, :], in_=xsrc[:, i, :])),
                   waits=w, inc=s_xld2, inc_n=16)
        if stage == 0:
            p.op("sync", L(lambda e, i=i, xv_=xv_: e.dma_start(
                out=xdst[:, i, :], in_=xtv[:, i % 2, :])),
                waits=[(s_xld2, xv_), (s_post, 3)], inc=s_sto, inc_n=16)
            continue
        def fap(e, i=i, xv_=xv_, stage=stage):
            sl = i % 2
            if stage in (12, 1, 2) and False:
                pass
            if stage == 10:
                p.ss("vector", e, e.tensor_scalar(
                    out=xnfv[:, i % 4, :], in0=xtv[:, sl, :],
                    scalar1=srstd[:, i:i + 1], scalar2=snmu[:, i:i + 1],
                    op0=ALU.mult, op1=ALU.add))
                return e.tensor_copy(out=xnbv[:, i % 2, :], in_=xnfv[:, i % 4, :])
            if stage == 11:
                p.ss("vector", e, e.tensor_scalar(
                    out=xnfv[:, i % 4, :], in0=xtv[:, sl, :],
                    scalar1=srstd[:, i:i + 1], scalar2=snmu[:, i:i + 1],
                    op0=ALU.mult, op1=ALU.add))
                p.ss("vector", e, e.tensor_tensor(out=xnfv[:, i % 4, :], in0=xnfv[:, i % 4, :],
                                                  in1=gam_rep[:], op=ALU.mult))
                p.ss("vector", e, e.tensor_tensor(out=xnfv[:, i % 4, :], in0=xnfv[:, i % 4, :],
                                                  in1=bet_rep[:], op=ALU.add))
                return e.tensor_copy(out=xnbv[:, i % 2, :], in_=xnfv[:, i % 4, :])
            p.ss("vector", e, e.tensor_scalar(
                out=xnfv[:, i % 4, :], in0=xtv[:, sl, :],
                scalar1=srstd[:, i:i + 1], scalar2=snmu[:, i:i + 1],
                op0=ALU.mult, op1=ALU.add))
            p.ss("vector", e, e.tensor_tensor(out=xnfv[:, i % 4, :], in0=xnfv[:, i % 4, :],
                                              in1=gam_rep[:], op=ALU.mult))
            p.ss("vector", e, e.tensor_tensor(out=xnfv[:, i % 4, :], in0=xnfv[:, i % 4, :],
                                              in1=bet_rep[:], op=ALU.add))
            p.ss("vector", e, e.tensor_scalar(
                out=escr[:, 0:512], in0=xtv[:, sl, 0:512],
                scalar1=srstd_o[:, i:i + 1], scalar2=snmu_o[:, i:i + 1],
                op0=ALU.mult, op1=ALU.add))
            p.ss("vector", e, e.tensor_add(out=xtv[:, sl, 0:512], in0=xtv[:, sl, 0:512],
                                           in1=escr[:, 0:512]))
            p.ss("vector", e, e.tensor_scalar(
                out=escr[:, 0:512], in0=xtv[:, sl, 512:1024],
                scalar1=srstd_o[:, i:i + 1], scalar2=snmu_o[:, i:i + 1],
                op0=ALU.mult, op1=ALU.add))
            p.ss("vector", e, e.tensor_add(out=xtv[:, sl, 512:1024], in0=xtv[:, sl, 512:1024],
                                           in1=escr[:, 0:512]))
            p.ss("vector", e, e.tensor_tensor(out=escr[:, 0:512], in0=dsk_rep[:, 0:512],
                                              in1=xnfv[:, i % 4, 0:512], op=ALU.mult))
            p.ss("vector", e, e.tensor_add(out=xtv[:, sl, 0:512], in0=xtv[:, sl, 0:512],
                                           in1=escr[:, 0:512]))
            p.ss("vector", e, e.tensor_tensor(out=escr[:, 0:512], in0=dsk_rep[:, 512:1024],
                                              in1=xnfv[:, i % 4, 512:1024], op=ALU.mult))
            p.ss("vector", e, e.tensor_add(out=xtv[:, sl, 512:1024], in0=xtv[:, sl, 512:1024],
                                           in1=escr[:, 0:512]))
            return e.tensor_copy(out=xnbv[:, i % 2, :], in_=xnfv[:, i % 4, :])
        w = [(s_xld2, xv_), (s_post, 3)]
        if i >= 2 and stage >= 13:
            w.append((s_tr, 4 * (i - 2) + 4))
        p.op("vector", fap, waits=w, inc=s_apply, inc_n=4)
        p.op("sync", L(lambda e, i=i: e.dma_start(out=xdst[:, i, :], in_=xtv[:, i % 2, :])),
             waits=[(s_apply, s_apply.v)], inc=s_sto, inc_n=16)
        if stage in (10, 11, 12):
            continue
        for c4 in (0, 4):
            par = tb_cnt[0] % 2
            tb_cnt[0] += 1
            w = [(s_apply, s_apply.v)]
            if tb_ev[0] >= 2:
                w.append((s_trevb, tb_ev[0] - 1))
            def ftrb(e, i=i, c4=c4, par=par):
                inst = None
                for j in range(4):
                    inst = e.transpose(ptrb[par][:, j * 128:(j + 1) * 128],
                                       xnbv[:, i % 2, (c4 + j) * 128:(c4 + j + 1) * 128],
                                       ident_b[:])
                return inst
            p.op("tensor", ftrb, waits=w, inc=s_tr)
            w2 = [(s_tr, s_tr.v)]
            if i >= 4 and c4 == 0:
                w2.append((s_dj, 9 * ch))
            tb_ev[0] += 1
            p.op("scalar", L(lambda e, i=i, c4=c4, par=par: e.activation(
                out=xnTv[:, c4:c4 + 4, (i % 4) * 128:(i % 4 + 1) * 128],
                in_=ptrb[par][:].rearrange("p (c t) -> p c t", t=128),
                func=AF.Copy)), waits=w2, inc=s_trevb)
            if stage == 14:
                continue
            parf = tf_cnt[0] % 2
            tf_cnt[0] += 1
            wf = [(s_apply, s_apply.v)]
            if tf_ev[0] >= 2:
                wf.append((s_trevf, tf_ev[0] - 1))
            def ftrf(e, i=i, c4=c4, parf=parf):
                inst = None
                for j in range(4):
                    inst = e.transpose(ptrf[parf][:, j * 128:(j + 1) * 128],
                                       xnfv[:, i % 4, (c4 + j) * 128:(c4 + j + 1) * 128],
                                       ident_f[:])
                return inst
            p.op("tensor", ftrf, waits=wf, inc=s_tr)
            tf_ev[0] += 1
            p.op("vector", L(lambda e, i=i, c4=c4, parf=parf: e.tensor_copy(
                out=xnTfv[:, c4:c4 + 4, (i % 4) * 128:(i % 4 + 1) * 128],
                in_=ptrf[parf][:].rearrange("p (c t) -> p c t", t=128))),
                waits=[(s_tr, s_tr.v)], inc=s_trevf)
        if i % 4 != 3 or stage < 2:
            continue
        for mc in range(NCH):
            par = s_dj.v % 2
            def fdj(e, mc=mc, par=par):
                inst = None
                for c in range(NCH):
                    inst = e.matmul(pq[par][:], wdv[:, c, mc * 128:(mc + 1) * 128],
                                    xnTv[:, c, :], start=(c == 0), stop=(c == NCH - 1))
                return inst
            w = [(s_trevb, s_trevb.v)]
            if s_dj.v >= 2:
                w.append((s_dje, s_dje.v - 1))
            p.op("tensor", fdj, waits=w, inc=s_dj)
            def fdje(e, mc=mc, par=par, ch=ch):
                p.ss("scalar", e, e.activation(out=escr2[:], in_=pq[par][:], func=AF.Exp,
                                               bias=bd_col[:, mc:mc + 1]))
                return e.activation(out=dTv[:, mc, :], in_=escr2[:], func=AF.Ln, bias=1.0)
            w2 = [(s_dj, s_dj.v)]
            if ch >= 1:
                w2.append((s_gx, 16 * ch))
            p.op("scalar", fdje, waits=w2, inc=s_dje)
        par = s_dj.v % 2
        def fbc(e, par=par):
            inst = None
            for c in range(NCH):
                inst = e.matmul(pq[par][0:32, 0:512], wbcv[:, c, :], xnTv[:, c, :],
                                start=(c == 0), stop=(c == NCH - 1))
            return inst
        w = [(s_trevb, s_trevb.v)]
        if s_dj.v >= 2:
            w.append((s_dje, s_dje.v - 1))
        p.op("tensor", fbc, waits=w, inc=s_dj)
        p.op("scalar", L(lambda e, par=par: e.activation(
            out=escr2[0:32, :], in_=pq[par][0:32, 0:512], func=AF.Identity,
            bias=bbc_sb[:])), waits=[(s_dj, s_dj.v)], inc=s_dje)
        w2 = [(s_dje, s_dje.v)]
        if ch >= 1:
            w2.append((s_bcd, 33 * 16 * ch))
        p.op("sync", L(lambda e: e.dma_start(out=bc_scr, in_=escr2[0:32, 0:512])),
             waits=w2, inc=s_bcd, inc_n=16)
        def fdx(e, ch=ch):
            inst = None
            for c in range(NCH):
                inst = e.tensor_tensor(out=dxTv[:, c, :], in0=dTv[:, c, :],
                                       in1=xnTfv[:, c, :], op=ALU.mult)
            return inst
        p.op("vector", fdx, waits=[(s_dje, s_dje.v), (s_trevf, s_trevf.v)], inc=s_sc)
        sc_base = s_sc.v
        for n in range(NST):
            def fg(e, n=n, ch=ch):
                inst = None
                for c in range(NCH):
                    inst = e.activation(out=gv[:, c, :], in_=dTv[:, c, :],
                                        func=AF.Exp, scale=Av[:, c, n:n + 1])
                return inst
            w = [(s_dje, s_dje.v), (s_sc, sc_base)]
            if n >= 1:
                w.append((s_sc, s_sc.v - 1))
            p.op("scalar", fg, waits=w, inc=s_gx)
            p.op("scalar", L(lambda e, n=n, ch=ch: e.dma_start(
                out=brow[0:1, (n % 2) * 512:(n % 2 + 1) * 512], in_=bc_scr[n:n + 1, :])),
                waits=[(s_bcd, s_bcd.v)], inc=s_bcd, inc_n=16)
            p.op("scalar", L(lambda e, n=n, ch=ch: e.dma_start(
                out=crow[0:1, (n % 2) * 512:(n % 2 + 1) * 512], in_=bc_scr[16 + n:17 + n, :])),
                waits=[], inc=s_bcd, inc_n=16)
            rpar = n % 2
            w2 = [(s_bcd, s_bcd.v)]
            if n >= 2:
                w2.append((s_sc, s_sc.v - 2))
            p.op("tensor", L(lambda e, n=n, rpar=rpar: e.matmul(
                prpl[rpar][:], ones_f[0:1, :], brow[0:1, (n % 2) * 512:(n % 2 + 1) * 512],
                start=True, stop=True)), waits=w2, inc=s_yt)
            def fsc(e, n=n, rpar=rpar, ch=ch):
                for c in range(NCH):
                    p.ss("vector", e, e.tensor_tensor(out=uv[:, c, :], in0=dxTv[:, c, :],
                                                      in1=prpl[rpar][:], op=ALU.mult))
                for c in range(NCH):
                    p.ss("vector", e, e.tensor_tensor_scan(
                        out=hv[:, c, :], data0=gv[:, c, :], data1=uv[:, c, :],
                        initial=hcv[:, c, n:n + 1], op0=ALU.mult, op1=ALU.add))
                return e.tensor_copy(out=hcv[:, :, n:n + 1],
                                     in_=hv[:, :, 511:512])
            p.op("vector", fsc, waits=[(s_gx, s_gx.v), (s_yt, s_yt.v)], inc=s_sc)
            p.op("tensor", L(lambda e, n=n, rpar=rpar: e.matmul(
                prpl[rpar][:], ones_f[0:1, :], crow[0:1, (n % 2) * 512:(n % 2 + 1) * 512],
                start=True, stop=True)), waits=[(s_sc, s_sc.v)], inc=s_yt)
            def fy(e, n=n, rpar=rpar):
                inst = None
                for c in range(NCH):
                    p.ss("vector", e, e.tensor_tensor(out=uv[:, c, :], in0=hv[:, c, :],
                                                      in1=prpl[rpar][:], op=ALU.mult))
                    inst = e.tensor_add(out=yv[:, c, :], in0=yv[:, c, :], in1=uv[:, c, :])
                    if c < NCH - 1:
                        p.ss("vector", e, inst)
                return inst
            p.op("vector", fy, waits=[(s_yt, s_yt.v)], inc=s_sc)
        for ti4 in range(4):
            gt = 4 * ch + ti4
            w = []
            if gt >= 2:
                w.append((s_sto2, 16 * (gt - 1)))
            xv2 = p.op("sync", L(lambda e, gt=gt: e.dma_start(
                out=xsv[:, gt % 2, :], in_=xosrc[:, gt, :])),
                waits=[(s_sto, 16 * (gt + 1))] + w, inc=s_xs, inc_n=16)
            for c4 in (0, 4):
                parf = tf_cnt[0] % 2
                tf_cnt[0] += 1
                w = [(s_sc, s_sc.v)]
                if tf_ev[0] >= 2:
                    w.append((s_trevf, tf_ev[0] - 1))
                def fyt(e, ti4=ti4, c4=c4, parf=parf):
                    inst = None
                    for j in range(4):
                        inst = e.transpose(ptrf[parf][:, j * 128:(j + 1) * 128],
                                           yv[:, c4 + j, ti4 * 128:(ti4 + 1) * 128],
                                           ident_f[:])
                    return inst
                p.op("tensor", fyt, waits=w, inc=s_tr)
                tf_ev[0] += 1
                p.op("vector", L(lambda e, gt=gt, c4=c4, parf=parf, xv2=xv2: e.scalar_tensor_tensor(
                    out=xsv[:, gt % 2, c4 * 128:(c4 + 4) * 128],
                    in0=ptrf[parf][:], scalar=1.0,
                    in1=xsv[:, gt % 2, c4 * 128:(c4 + 4) * 128],
                    op0=ALU.mult, op1=ALU.add)),
                    waits=[(s_tr, s_tr.v), (s_xs, xv2)], inc=s_trevf)
            p.op("sync", L(lambda e, gt=gt: e.dma_start(
                out=xdst[:, gt, :], in_=xsv[:, gt % 2, :])),
                waits=[(s_trevf, s_trevf.v)], inc=s_sto2, inc_n=16)
        p.op("vector", lambda e: e.memset(y_sb[:], 0.0),
             waits=[(s_tr, s_tr.v)], inc=s_sc)
    p.op("sync", lambda e: None, waits=[(s_sto2, s_sto2.v)])
    return (s_sto2, s_sto2.v)


def build_prog_ssm(stage=2):
    nc = bass.Bass("TRN2", target_bir_lowering=False)
    ctx = ExitStack()
    x_in = nc.dram_tensor("x_in", [T, D], F32, kind="ExternalInput").ap()
    gam = nc.dram_tensor("gam", [1, D], F32, kind="ExternalInput").ap()
    bet = nc.dram_tensor("bet", [1, D], F32, kind="ExternalInput").ap()
    Wd = nc.dram_tensor("Wd", [D, D], F32, kind="ExternalInput").ap()
    bd = nc.dram_tensor("bd", [1, D], F32, kind="ExternalInput").ap()
    Wbc = nc.dram_tensor("Wbc", [D, 32], F32, kind="ExternalInput").ap()
    bbc = nc.dram_tensor("bbc", [32, 1], F32, kind="ExternalInput").ap()
    Alog = nc.dram_tensor("Alog", [D, 16], F32, kind="ExternalInput").ap()
    Dsk = nc.dram_tensor("Dsk", [1, D], F32, kind="ExternalInput").ap()
    identb = nc.dram_tensor("identb", [128, 128], BF16, kind="ExternalInput").ap()
    identf = nc.dram_tensor("identf", [128, 128], F32, kind="ExternalInput").ap()
    onesb = nc.dram_tensor("onesb", [1, 128], BF16, kind="ExternalInput").ap()
    x_out = nc.dram_tensor("x_out", [T, D], F32, kind="ExternalOutput").ap()
    bc_scr = nc.dram_tensor("bc_scr", [32, 512], F32).ap()

    p = Prog(nc, ctx)
    emit_ssm(p, nc, ctx, x_in=x_in, gam=gam, bet=bet, Wd=Wd, bd=bd, Wbc=Wbc, bbc=bbc, Alog=Alog, Dsk=Dsk, identb=identb, identf=identf, onesb=onesb, x_out=x_out, bc_scr=bc_scr, stage=stage)
    p.build()
    ctx.close()
    return nc



_PROGS = {}


def _prog(name):
    if name not in _PROGS:
        _PROGS[name] = {
            "c12": build_prog_c12_sliced,
            "m": build_prog_m_sliced,
            "attn_a": build_prog_attn_a,
            "ffn0": build_prog_ffn0,
            "attn_b": build_prog_attn_b,
            "slstm": build_prog_slstm,
            "l1ffn": build_prog_l1ffn,
            "ssm": build_prog_ssm,
        }[name]()
    return _PROGS[name]


def _extra_consts():
    c = get_consts()
    if "amasks" not in c:
        r_off = np.array([0, 128, 256, 384])
        am = np.zeros((4, 128, 512), np.float32)
        for ri, r in enumerate(r_off):
            am[ri] = (np.arange(512)[None, :] >= (np.arange(128)[:, None] + r))
        c["amasks"] = _bf(am.reshape(512, 512))
        k_ = np.arange(128)[:, None]
        q_ = np.arange(128)[None, :]
        tri = (q_ >= k_).astype(np.float32)
        low = (q_ < k_).astype(np.float32)
        s2m = ((q_ < k_) | (k_ < 64)).astype(np.float32)
        c["bmasks"] = _bf(np.concatenate([tri, low, s2m], 0))
        HD = 64
        inv = 1.0 / 10000.0 ** (np.arange(0, HD, 2) / HD)
        fr = np.arange(T)[:, None] * inv[None, :]
        cosT = np.cos(fr).astype(np.float32)
        sinT = np.sin(fr).astype(np.float32)
        c["ctok"] = cosT.reshape(32, 128, 32).transpose(1, 0, 2).reshape(128, 32 * 32).copy()
        c["stok"] = sinT.reshape(32, 128, 32).transpose(1, 0, 2).reshape(128, 32 * 32).copy()
        c["iota64f"] = np.broadcast_to(np.arange(64, dtype=np.float32), (128, 64)).copy()
    return c


def _run(name, maps):
    res = run_bass_kernel_spmd(_prog(name), maps, core_ids=list(range(8)))
    return res.results


def kernel(**ins):
    c = _extra_consts()
    x = np.ascontiguousarray(ins["L0_attn_a_x"].astype(np.float32))

    Wr, Wi = ins["L0_ffn_w_freq_real"], ins["L0_ffn_w_freq_imag"]
    maps = []
    for core in range(8):
        sl = slice(core * 128, (core + 1) * 128)
        m = {n: c[n] for n in ["FrH", "FrL", "FinH", "FinL", "FiH", "FiL"]}
        m["Wrs"] = np.ascontiguousarray(Wr[:, sl])
        m["Wis"] = np.ascontiguousarray(Wi[:, sl])
        maps.append(m)
    r = _run("c12", maps)
    C1 = np.concatenate([r[i]["C1o"] for i in range(8)], 1)
    C2 = np.concatenate([r[i]["C2o"] for i in range(8)], 1)
    maps = []
    for core in range(8):
        sl = slice(core * 128, (core + 1) * 128)
        maps.append(dict(C1=C1, C2=C2, identf=c["identf"],
                         GrHs=np.ascontiguousarray(c["GrH"][:, sl]),
                         GrLs=np.ascontiguousarray(c["GrL"][:, sl]),
                         GinHs=np.ascontiguousarray(c["GinH"][:, sl]),
                         GinLs=np.ascontiguousarray(c["GinL"][:, sl])))
    r = _run("m", maps)
    M = np.ascontiguousarray(np.concatenate([r[i]["Mo"] for i in range(8)], 1))

    maps = [dict(x_in=x[i], w_qkv=ins["L0_attn_a_w_qkv"],
                 b_qkv=ins["L0_attn_a_b_qkv"][None, :],
                 w_proj=ins["L0_attn_a_w_proj"], b_proj=ins["L0_attn_a_b_proj"][None, :],
                 identb=c["identb"], onesb=c["ones_row_b"], amasks=c["amasks"])
            for i in range(8)]
    r = _run("attn_a", maps)
    x = np.stack([r[i]["x_out"] for i in range(8)])

    maps = [dict(x_in=x[i], M=M, bias=ins["L0_ffn_bias"][None, :],
                 identb=c["identb"], onesb=c["ones_row_b"]) for i in range(8)]
    r = _run("ffn0", maps)
    x = np.stack([r[i]["x_out"] for i in range(8)])

    maps = [dict(x_in=x[i], Wq=ins["L0_attn_b_Wq"], Wk=ins["L0_attn_b_Wk"],
                 Wv=ins["L0_attn_b_Wv"], Wo=ins["L0_attn_b_Wo"],
                 identb=c["identb"], bmasks=c["bmasks"], ctok=c["ctok"], stok=c["stok"])
            for i in range(8)]
    r = _run("attn_b", maps)
    x = np.stack([r[i]["x_out"] for i in range(8)])

    maps = [dict(x_in=x[i], M=M, bias=ins["L0_ffn_bias"][None, :],
                 identb=c["identb"], onesb=c["ones_row_b"]) for i in range(8)]
    r = _run("ffn0", maps)
    x = np.stack([r[i]["x_out"] for i in range(8)])

    maps = [dict(x_in=x[i], lnw=ins["L1_attn_a_ln_w"][None, :],
                 lnb=ins["L1_attn_a_ln_b"][None, :],
                 Wi=ins["L1_attn_a_Wi"], Wf=ins["L1_attn_a_Wf"],
                 Wz=ins["L1_attn_a_Wz"], Wo=ins["L1_attn_a_Wo"],
                 bi=ins["L1_attn_a_bi"][:, None], bf=ins["L1_attn_a_bf"][:, None],
                 bz=ins["L1_attn_a_bz"][:, None], bo=ins["L1_attn_a_bo"][:, None],
                 Wp=ins["L1_attn_a_Wp"], bp=ins["L1_attn_a_bp"][None, :],
                 identb=c["identb"], onesb=c["ones_row_b"]) for i in range(8)]
    r = _run("slstm", maps)
    x = np.stack([r[i]["x_out"] for i in range(8)])

    keys_r = np.ascontiguousarray(ins["L1_ffn_keys"].transpose(0, 2, 1, 3)).reshape(1024, 32)
    maps = [dict(x_in=x[i], wqr=ins["L1_ffn_w_queries"], keys=keys_r,
                 de_t=ins["L1_ffn_down_embed"], ue_t=ins["L1_ffn_up_embed"],
                 wg_r=ins["L1_ffn_w_gate"], wu_r=ins["L1_ffn_w_up"],
                 wd_r=ins["L1_ffn_w_down"], identb=c["identb"], iota64=c["iota64f"])
            for i in range(8)]
    r = _run("l1ffn", maps)
    x = np.stack([r[i]["x_out"] for i in range(8)])

    Wbc = np.ascontiguousarray(np.concatenate(
        [ins["L1_attn_b_W_B"], ins["L1_attn_b_W_C"]], 1))
    bbc = np.ascontiguousarray(np.concatenate(
        [ins["L1_attn_b_b_B"], ins["L1_attn_b_b_C"]])[:, None])
    maps = [dict(x_in=x[i], gam=ins["L1_attn_b_gamma"][None, :],
                 bet=ins["L1_attn_b_beta"][None, :],
                 Wd=ins["L1_attn_b_W_delta"], bd=ins["L1_attn_b_b_delta"][None, :],
                 Wbc=Wbc, bbc=bbc, Alog=ins["L1_attn_b_A_log"],
                 Dsk=ins["L1_attn_b_D_skip"][None, :],
                 identb=c["identb"], identf=c["identf"], onesb=c["ones_row_b"])
            for i in range(8)]
    r = _run("ssm", maps)
    x = np.stack([r[i]["x_out"] for i in range(8)])

    maps = [dict(x_in=x[i], wqr=ins["L1_ffn_w_queries"], keys=keys_r,
                 de_t=ins["L1_ffn_down_embed"], ue_t=ins["L1_ffn_up_embed"],
                 wg_r=ins["L1_ffn_w_gate"], wu_r=ins["L1_ffn_w_up"],
                 wd_r=ins["L1_ffn_w_down"], identb=c["identb"], iota64=c["iota64f"])
            for i in range(8)]
    r = _run("l1ffn", maps)
    x = np.stack([r[i]["x_out"] for i in range(8)])
    return x.astype(np.float32)



def emit_mbuild(p, nc, ctx, Wr, Wi, Fp, Gp, CT, Mbuf, identb, xdbg=None):
    L = lambda fn: fn
    s_ld = p.gsem("mb_ld")
    s_w = p.sem("mb_w")
    s_mm = p.sem("mb_mm")
    s_ev = p.sem("mb_ev")
    s_tr = p.sem("mb_tr")
    s_te = p.sem("mb_te")
    s_cst = p.sem("mb_cst")
    s_gld = p.gsem("mb_gld")
    s_mm2 = p.sem("mb_mm2")
    s_ev2 = p.sem("mb_ev2")
    s_mst = p.sem("mb_mst")

    ident_b = sbt(nc, ctx, "mb_ident", [128, 128], BF16)
    wbuf = {}
    for n, ap in (("r", Wr), ("i", Wi)):
        wbuf[n + "f"] = sbt(nc, ctx, f"mb_w{n}f", [128, NCH * D], F32)
        wbuf[n + "h"] = sbt(nc, ctx, f"mb_w{n}h", [128, NCH * D], BF16)
        wbuf[n + "l"] = sbt(nc, ctx, f"mb_w{n}l", [128, NCH * D], BF16)
    ftiles = sbt(nc, ctx, "mb_ft", [128, 48 * 128], BF16)
    cev = sbt(nc, ctx, "mb_cev", [128, D], F32)
    chl = sbt(nc, ctx, "mb_chl", [128, 2 * D], BF16)
    wscr = sbt(nc, ctx, "mb_wscr", [128, NCH * D // 2], F32)
    gtiles = sbt(nc, ctx, "mb_gt", [128, 4 * 1024], BF16)
    ctiles = sbt(nc, ctx, "mb_ct", [128, 4 * 128], BF16)
    chl2 = sbt(nc, ctx, "mb_chl2", [128, 2 * D], BF16)
    mev = sbt(nc, ctx, "mb_mev", [128, D], F32)
    pqm = [pst(nc, ctx, f"pq{k}", [128, 512]) for k in range(2)]
    ptrm = [pst(nc, ctx, f"ptr{k}", [128, 512], BF16) for k in range(2)]

    p.op("gpsimd", lambda e: e.dma_start(out=ident_b[:], in_=identb), inc=s_ld, inc_n=16)
    for n, ap in (("r", Wr), ("i", Wi)):
        p.op("gpsimd", L(lambda e, n=n, ap=ap: e.dma_start(
            out=wbuf[n + "f"][:].rearrange("p (c m) -> p c m", m=D),
            in_=ap.rearrange("(c p) m -> p c m", p=128))), inc=s_ld, inc_n=16)
    def fsplit(e):
        for n in ("r", "i"):
            p.ss("vector", e, e.tensor_copy(out=wbuf[n + "h"][:], in_=wbuf[n + "f"][:]))
            p.ss("vector", e, e.tensor_copy(out=wscr[:], in_=wbuf[n + "h"][:].rearrange("p (a m) -> p a m", a=2)[:, 0, :]))
            p.ss("vector", e, e.tensor_sub(out=wscr[:], in0=wbuf[n + "f"][:].rearrange("p (a m) -> p a m", a=2)[:, 0, :], in1=wscr[:]))
            p.ss("vector", e, e.tensor_copy(out=wbuf[n + "l"][:].rearrange("p (a m) -> p a m", a=2)[:, 0, :], in_=wscr[:]))
            p.ss("vector", e, e.tensor_copy(out=wscr[:], in_=wbuf[n + "h"][:].rearrange("p (a m) -> p a m", a=2)[:, 1, :]))
            p.ss("vector", e, e.tensor_sub(out=wscr[:], in0=wbuf[n + "f"][:].rearrange("p (a m) -> p a m", a=2)[:, 1, :], in1=wscr[:]))
            p.ss("vector", e, e.tensor_copy(out=wbuf[n + "l"][:].rearrange("p (a m) -> p a m", a=2)[:, 1, :], in_=wscr[:]))
        return e.engine_nop()
    p.op("vector", fsplit, waits=[(s_ld, s_ld.v)], inc=s_w)

    fnames = ["FrH", "FrL", "FinH", "FinL", "FiH", "FiL"]
    terms1 = [(0, "r", "h"), (1, "r", "h"), (0, "r", "l"),
              (2, "i", "h"), (3, "i", "h"), (2, "i", "l")]
    terms2 = [(0, "i", "h"), (1, "i", "h"), (0, "i", "l"),
              (4, "r", "h"), (5, "r", "h"), (4, "r", "l")]
    ftv = ftiles[:].rearrange("p (g m) -> p g m", m=128)
    chv = chl[:].rearrange("p (a m) -> p a m", m=D)
    for mc in range(NCH):
        for fi, fn_ in enumerate(fnames):
            w = []
            if mc >= 1 and fi == 0:
                w.append((s_mm, 4 * mc))
            p.op("gpsimd", L(lambda e, fi=fi, fn_=fn_, mc=mc: e.dma_start(
                out=ftv[:, fi * 8:(fi + 1) * 8, :],
                in_=Fp[fn_].rearrange("(c p) m -> p c m", p=128)[:, :, mc * 128:(mc + 1) * 128])),
                waits=w, inc=s_ld, inc_n=16)
        for ci, terms in ((0, terms1), (1, terms2)):
            for jh in range(2):
                par = s_mm.v % 2
                def fmm(e, terms=terms, jh=jh, par=par, mc=mc):
                    first = True
                    inst = None
                    nterm = len(terms) * NCH
                    k = 0
                    for (fi, wn, hl) in terms:
                        for c in range(NCH):
                            k += 1
                            inst = e.matmul(
                                pqm[par][:],
                                ftv[:, fi * 8 + c, :],
                                wbuf[wn + hl][:].rearrange("p (c m) -> p c m", m=D)[:, c, jh * 512:(jh + 1) * 512],
                                start=first, stop=(k == nterm))
                            first = False
                    return inst
                w = [(s_w, 1), (s_ld, s_ld.v)]
                if s_mm.v >= 2:
                    w.append((s_ev, s_ev.v - 1))
                p.op("tensor", fmm, waits=w, inc=s_mm)
                p.op("vector", L(lambda e, jh=jh, par=par: e.tensor_copy(
                    out=cev[:, jh * 512:(jh + 1) * 512], in_=pqm[par][:])),
                    waits=[(s_mm, s_mm.v)], inc=s_ev)
            def fchl(e, ci=ci):
                p.ss("vector", e, e.tensor_copy(out=chv[:, 0, :], in_=cev[:]))
                p.ss("vector", e, e.tensor_copy(out=wscr[:, 0:D], in_=chv[:, 0, :]))
                p.ss("vector", e, e.tensor_sub(out=wscr[:, 0:D], in0=cev[:], in1=wscr[:, 0:D]))
                return e.tensor_copy(out=chv[:, 1, :], in_=wscr[:, 0:D])
            p.op("vector", fchl, waits=[(s_ev, s_ev.v)], inc=s_ev)
            for hl in range(2):
                for kc in range(NCH):
                    par = s_tr.v % 2
                    w = [(s_ev, s_ev.v)]
                    if s_te.v >= 2:
                        w.append((s_te, s_te.v - 1))
                    p.op("tensor", L(lambda e, hl=hl, kc=kc, par=par: e.transpose(
                        ptrm[par][:, 0:128], chv[:, hl, kc * 128:(kc + 1) * 128], ident_b[:])),
                        waits=w, inc=s_tr)
                    p.op("scalar", L(lambda e, hl=hl, kc=kc, par=par: e.activation(
                        out=chl2[:, (hl * 8 + kc) * 128:(hl * 8 + kc + 1) * 128],
                        in_=ptrm[par][:, 0:128], func=AF.Copy)),
                        waits=[(s_tr, s_tr.v)], inc=s_te)
            names = [("C1Th", "C1Tl"), ("C2Th", "C2Tl")][ci]
            for hl in range(2):
                p.op("sync", L(lambda e, ci=ci, hl=hl, mc=mc, names=names: e.dma_start(
                    out=CT[names[hl]].rearrange("(c p) m -> p c m", p=128)[:, :, mc * 128:(mc + 1) * 128],
                    in_=chl2[:].rearrange("p (a m) -> p a m", m=1024)[:, hl, :].rearrange("p (c m) -> p c m", m=128))),
                    waits=[(s_te, s_te.v)], inc=s_cst, inc_n=16)
    t2 = [("C1Th", "GrH"), ("C1Th", "GrL"), ("C1Tl", "GrH"),
          ("C2Th", "GinH"), ("C2Th", "GinL"), ("C2Tl", "GinH")]
    gv = gtiles[:].rearrange("p (g m) -> p g m", m=1024)
    cv2 = ctiles[:].rearrange("p (g m) -> p g m", m=128)
    gnames = ["GrH", "GrL", "GinH", "GinL"]
    cnames = ["C1Th", "C1Tl", "C2Th", "C2Tl"]
    for mc2 in range(NCH):
        for kc in range(NCH):
            w = [(s_cst, s_cst.v)]
            if not (mc2 == 0 and kc == 0):
                w.append((s_mm2, s_mm2.v))
            for gi_, gn in enumerate(gnames):
                p.op("gpsimd", L(lambda e, gi_=gi_, gn=gn, kc=kc: e.dma_start(
                    out=gv[:, gi_, :],
                    in_=Gp[gn].rearrange("(c p) m -> p c m", p=128)[:, kc, :])),
                    waits=(w if gi_ == 0 else []), inc=s_gld, inc_n=16)
            for ci_, cn in enumerate(cnames):
                p.op("gpsimd", L(lambda e, ci_=ci_, cn=cn, kc=kc, mc2=mc2: e.dma_start(
                    out=cv2[:, ci_, :],
                    in_=CT[cn].rearrange("(c p) m -> p c m", p=128)[:, kc, mc2 * 128:(mc2 + 1) * 128])),
                    inc=s_gld, inc_n=16)
            for jh in range(2):
                def fmm2(e, kc=kc, jh=jh, mc2=mc2):
                    inst = None
                    for (cn, gn) in t2:
                        inst = e.matmul(pqm[jh][:],
                                        cv2[:, cnames.index(cn), :],
                                        gv[:, gnames.index(gn), jh * 512:(jh + 1) * 512],
                                        start=(kc == 0 and (cn, gn) == t2[0]),
                                        stop=(kc == NCH - 1 and (cn, gn) == t2[-1]))
                    return inst
                w = [(s_gld, s_gld.v)]
                if kc == 0 and s_mm2.v >= 2:
                    w.append((s_ev2, s_ev2.v))
                p.op("tensor", fmm2, waits=w, inc=s_mm2)
        for jh in range(2):
            p.op("vector", L(lambda e, jh=jh: e.tensor_copy(
                out=mev[:, jh * 512:(jh + 1) * 512], in_=pqm[jh][:])),
                waits=[(s_mm2, s_mm2.v - 1 + jh)], inc=s_ev2)
        p.op("sync", L(lambda e, mc2=mc2: e.dma_start(
            out=Mbuf.rearrange("(c p) m -> p c m", p=128)[:, mc2, :], in_=mev[:])),
            waits=[(s_ev2, s_ev2.v)], inc=s_mst, inc_n=16)
        if xdbg is not None:
            p.op("sync", L(lambda e, mc2=mc2: e.dma_start(
                out=xdbg.rearrange("(c p) m -> p c m", p=128)[:, mc2, :], in_=mev[:])),
                waits=[(s_ev2, s_ev2.v)], inc=s_mst, inc_n=16)
    p.op("sync", lambda e: None, waits=[(s_mst, s_mst.v)])
    return (s_mst, s_mst.v)



_FFN0A_STAGE = [2]


def build_prog_fused(nblocks=9, start=0):
    nc = bass.Bass("TRN2", target_bir_lowering=False)
    ctx = ExitStack()
    dt_ = lambda n, sh, dt=F32, kind=None: (
        nc.dram_tensor(n, sh, dt, kind=kind).ap() if kind else nc.dram_tensor(n, sh, dt).ap())
    I = lambda n, sh, dt=F32: dt_(n, sh, dt, "ExternalInput")

    x_in = I("x_in", [T, D])
    x_out = dt_("x_out", [T, D], F32, "ExternalOutput")
    dbg = dt_("dbg", [128, 96], F32, "ExternalOutput")
    xb0 = dt_("xb0", [T, D])
    xb1 = dt_("xb1", [T, D])
    Mbuf = dt_("Mbuf", [D, D])
    CT = {n: dt_(n, [D, D], BF16) for n in ("C1Th", "C1Tl", "C2Th", "C2Tl")}
    bc_scr = dt_("bc_scr", [32, 512])

    identb = I("identb", [128, 128], BF16)
    identf = I("identf", [128, 128])
    onesb = I("onesb", [1, 128], BF16)
    amasks = I("amasks", [4 * 128, 512], BF16)
    bmasks = I("bmasks", [3 * 128, 128], BF16)
    ctok = I("ctok", [128, NT * 32])
    stok = I("stok", [128, NT * 32])
    iota64 = I("iota64", [128, 64])
    Wr = I("Wr", [D, D]); Wi_ = I("Wi_", [D, D])
    Fp = {n: I(n, [D, D], BF16) for n in ("FrH", "FrL", "FinH", "FinL", "FiH", "FiL")}
    Gp = {n: I(n, [D, D], BF16) for n in ("GrH", "GrL", "GinH", "GinL")}
    w_qkv = I("w_qkv", [D, 3 * D]); b_qkv = I("b_qkv", [1, 3 * D])
    w_proj = I("w_proj", [D, D]); b_proj = I("b_proj", [1, D])
    ffbias = I("ffbias", [1, D])
    Wq = I("Wq", [D, D]); Wk = I("Wk", [D, 256]); Wv = I("Wv", [D, 256]); Wo = I("Wo", [D, D])
    slW = {g: I(f"sl_W{g}", [64, D]) for g in "ifzo"}
    slb = {g: I(f"sl_b{g}", [64, 1]) for g in "ifzo"}
    lnw = I("lnw", [1, D]); lnb = I("lnb", [1, D])
    Wp = I("Wp", [D, 64]); bp = I("bp", [1, D])
    wqr = I("wqr", [256, D]); keys = I("keys", [1024, 32])
    de_t = I("de_t", [16384, D]); ue_t = I("ue_t", [16384, D])
    wg_r = I("wg_r", [2048, D]); wu_r = I("wu_r", [2048, D]); wd_r = I("wd_r", [D, 2048])
    gam = I("gam", [1, D]); bet = I("bet", [1, D])
    Wd = I("Wd", [D, D]); bd = I("bd", [1, D])
    Wbc = I("Wbc", [D, 32]); bbc = I("bbc", [32, 1])
    Alog = I("Alog", [D, 16]); Dsk = I("Dsk", [1, D])

    p = Prog(nc, ctx)
    nc._fused_prog = p
    p.arena_init(53000)
    p.psum_init()
    p.pool_init()

    blocks = [
        ("mb", lambda xi, xo: emit_mbuild(p, nc, ctx, Wr, Wi_, Fp, Gp, CT, Mbuf, identb,
                                          xdbg=(x_out if nblocks == 1 else None))),
        ("attn_a", lambda xi, xo: emit_attn_a(p, nc, ctx, xi, w_qkv, b_qkv, w_proj,
                                              b_proj, identb, onesb, amasks, xo)),
        ("ffn0a", lambda xi, xo: emit_ffn0(p, nc, ctx, xi, Mbuf, ffbias, identb, onesb, xo,
                                           stage=_FFN0A_STAGE[0])),
        ("attn_b", lambda xi, xo: emit_attn_b(p, nc, ctx, xi, Wq, Wk, Wv, Wo, identb,
                                              bmasks, ctok, stok, xo, dbg=dbg)),
        ("ffn0b", lambda xi, xo: emit_ffn0(p, nc, ctx, xi, Mbuf, ffbias, identb, onesb, xo)),
        ("slstm", lambda xi, xo: emit_slstm(p, nc, ctx, xi, lnw, lnb, slW, slb, Wp, bp,
                                            identb, onesb, xo)),
        ("l1ffna", lambda xi, xo: emit_l1ffn(p, nc, ctx, xi, wqr, keys, de_t, ue_t,
                                             wg_r, wu_r, wd_r, identb, iota64, xo)),
        ("ssm", lambda xi, xo: emit_ssm(p, nc, ctx, xi, gam, bet, Wd, bd, Wbc, bbc,
                                        Alog, Dsk, identb, identf, onesb, xo, bc_scr)),
        ("l1ffnb", lambda xi, xo: emit_l1ffn(p, nc, ctx, xi, wqr, keys, de_t, ue_t,
                                             wg_r, wu_r, wd_r, identb, iota64, xo)),
    ]
    if isinstance(start, (list, tuple)):
        blocks = [blocks[i] for i in start]
    else:
        blocks = blocks[start:nblocks] if nblocks > 0 else [blocks[0]]
    cur = x_in
    nxt = [xb0, xb1]
    g = None
    for bi_, (bn, fn_) in enumerate(blocks):
        if bi_ > 0:
            p.new_block(g)
        xo = x_out if bi_ == len(blocks) - 1 else nxt[bi_ % 2]
        if bn == "mb":
            xo = None
        g = fn_(cur, xo)
        if bn != "mb":
            cur = xo
    p.build()
    ctx.close()
    return nc


_FUSED = [None]


def kernel(**ins):
    c = _extra_consts()
    x = np.ascontiguousarray(ins["L0_attn_a_x"].astype(np.float32))
    if _FUSED[0] is None:
        _FUSED[0] = build_prog_fused()
    keys_r = np.ascontiguousarray(ins["L1_ffn_keys"].transpose(0, 2, 1, 3)).reshape(1024, 32)
    Wbc = np.ascontiguousarray(np.concatenate([ins["L1_attn_b_W_B"], ins["L1_attn_b_W_C"]], 1))
    bbc = np.ascontiguousarray(np.concatenate([ins["L1_attn_b_b_B"], ins["L1_attn_b_b_C"]])[:, None])
    base = dict(
        identb=c["identb"], identf=c["identf"], onesb=c["ones_row_b"],
        amasks=c["amasks"], bmasks=c["bmasks"], ctok=c["ctok"], stok=c["stok"],
        iota64=c["iota64f"],
        Wr=ins["L0_ffn_w_freq_real"], Wi_=ins["L0_ffn_w_freq_imag"],
        FrH=c["FrH"], FrL=c["FrL"], FinH=c["FinH"], FinL=c["FinL"],
        FiH=c["FiH"], FiL=c["FiL"],
        GrH=c["GrH"], GrL=c["GrL"], GinH=c["GinH"], GinL=c["GinL"],
        w_qkv=ins["L0_attn_a_w_qkv"], b_qkv=ins["L0_attn_a_b_qkv"][None, :],
        w_proj=ins["L0_attn_a_w_proj"], b_proj=ins["L0_attn_a_b_proj"][None, :],
        ffbias=ins["L0_ffn_bias"][None, :],
        Wq=ins["L0_attn_b_Wq"], Wk=ins["L0_attn_b_Wk"], Wv=ins["L0_attn_b_Wv"],
        Wo=ins["L0_attn_b_Wo"],
        sl_Wi=ins["L1_attn_a_Wi"], sl_Wf=ins["L1_attn_a_Wf"],
        sl_Wz=ins["L1_attn_a_Wz"], sl_Wo=ins["L1_attn_a_Wo"],
        sl_bi=ins["L1_attn_a_bi"][:, None], sl_bf=ins["L1_attn_a_bf"][:, None],
        sl_bz=ins["L1_attn_a_bz"][:, None], sl_bo=ins["L1_attn_a_bo"][:, None],
        lnw=ins["L1_attn_a_ln_w"][None, :], lnb=ins["L1_attn_a_ln_b"][None, :],
        Wp=ins["L1_attn_a_Wp"], bp=ins["L1_attn_a_bp"][None, :],
        wqr=ins["L1_ffn_w_queries"], keys=keys_r,
        de_t=ins["L1_ffn_down_embed"], ue_t=ins["L1_ffn_up_embed"],
        wg_r=ins["L1_ffn_w_gate"], wu_r=ins["L1_ffn_w_up"], wd_r=ins["L1_ffn_w_down"],
        gam=ins["L1_attn_b_gamma"][None, :], bet=ins["L1_attn_b_beta"][None, :],
        Wd=ins["L1_attn_b_W_delta"], bd=ins["L1_attn_b_b_delta"][None, :],
        Wbc=Wbc, bbc=bbc, Alog=ins["L1_attn_b_A_log"],
        Dsk=ins["L1_attn_b_D_skip"][None, :],
    )
    maps = [dict(base, x_in=x[i]) for i in range(8)]
    res = run_bass_kernel_spmd(_FUSED[0], maps, core_ids=list(range(8)))
    out = np.stack([res.results[i]["x_out"] for i in range(8)])
    return out.astype(np.float32)
